# version 36
# baseline (speedup 1.0000x reference)
import numpy as np
import ml_dtypes
import concourse.bass as bass
import concourse.mybir as mybir
from concourse.bass_utils import run_bass_kernel_spmd
from contextlib import ExitStack

F32 = mybir.dt.float32
BF16 = mybir.dt.bfloat16
U8 = mybir.dt.uint8
AF = mybir.ActivationFunctionType
ALU = mybir.AluOpType

B, S, D, H, HD = 4, 8192, 1024, 16, 64
TOK = 4096
NST1 = 16
NST2 = 8
EPS = 1e-5


class Res:
    __slots__ = ("name", "writer", "readers")

    def __init__(self, name):
        self.name = name
        self.writer = None
        self.readers = []


class MK:

    NLANES = 8

    def __init__(self, nc, ctx):
        self.nc = nc
        self.blocks = []
        self.cur = None
        self.cnt = {"T": 0, "V": 0, "S": 0, "G": 0}
        self.sem = {k: ctx.enter_context(nc.semaphore("sem" + k)) for k in self.cnt}
        for q in ("Q", "S", "G"):
            for i in range(self.NLANES):
                name = f"{q}L{i}"
                self.sem[name] = ctx.enter_context(nc.semaphore(name))
                self.cnt[name] = 0
        self.lane_rr = {"Q": 0, "S": 0, "G": 0}
        self.waited = {}
        self.bank_res = [Res(f"bank{i}") for i in range(8)]
        self.emitted = 0
        self.uid = 0
        self.serial = None

    def new_block(self):
        self.cur = {"T": [], "V": [], "S": [], "Q": [], "G": []}
        self.blocks.append(self.cur)
        self.waited = {}

    _ENG_SEM = {"T": "T", "V": "V", "S": "S", "G": "G"}

    serial_engines = None

    def group(self, eng, fn, reads=(), writes=(), dma_n=0):
        if self.serial is not None and (
                self.serial_engines is None
                or (not dma_n and eng in self.serial_engines)
                or (dma_n and ("Qd" if eng == "Q" else "Sd") in self.serial_engines)):
            reads = list(reads)
            writes = list(writes) + [self.serial]
        waits = {}

        def need(wv):
            if wv is None:
                return
            for sname, val in wv:
                if not dma_n and sname == self._ENG_SEM.get(eng):
                    continue
                key = (eng, sname)
                if self.waited.get(key, 0) >= val:
                    continue
                if waits.get(sname, 0) < val:
                    waits[sname] = val

        for r in reads:
            need(r.writer)
        for w in writes:
            need(w.writer)
            for rd in w.readers:
                need(rd)

        if dma_n:
            qname = eng if eng in ("Q", "G") else "S"
            lane_plan = []
            for _ in range(dma_n):
                li = self.lane_rr[qname]
                self.lane_rr[qname] = (li + 1) % self.NLANES
                lname = f"{qname}L{li}"
                prev = self.cnt[lname]
                self.cnt[lname] = prev + 16
                lane_plan.append((lname, prev, prev + 16))
            tag = [(ln, tv) for ln, pv, tv in lane_plan]
        else:
            semname = self._ENG_SEM[eng]
            self.cnt[semname] += 1
            tag = [(semname, self.cnt[semname])]

        wl = sorted(waits.items())
        for sname, val in wl:
            self.waited[(eng, sname)] = max(self.waited.get((eng, sname), 0), val)
        semmap = self.sem

        if dma_n:
            def closure(e, wl=wl, fn=fn, lane_plan=lane_plan):
                for sname, val in wl:
                    e.wait_ge(semmap[sname], val)
                k = [0]
                def dma(out_ap, in_ap):
                    lname, pv, tv = lane_plan[k[0]]
                    k[0] += 1
                    if pv > 0:
                        e.wait_ge(semmap[lname], pv)
                    e.dma_start(out_ap, in_ap).then_inc(semmap[lname], 16)
                fn(e, dma)
                assert k[0] == len(lane_plan)
        else:
            inc_sem = semmap[self._ENG_SEM[eng]]
            def closure(e, wl=wl, fn=fn, inc_sem=inc_sem):
                for sname, val in wl:
                    e.wait_ge(semmap[sname], val)
                fn(e).then_inc(inc_sem, 1)

        self.cur[eng].append(closure)
        for r in reads:
            r.readers.append(tag)
        for w in writes:
            w.writer = tag
            w.readers = []
        return tag

    def wait_all(self, eng, what):
        if what in ("Qd", "Sd", "Gd"):
            q = what[0]
            pairs = [(f"{q}L{i}", self.cnt[f"{q}L{i}"])
                     for i in range(self.NLANES)]
        else:
            pairs = [(what, self.cnt[what])]
        semmap = self.sem

        def w(e, pairs=pairs):
            for s, v in pairs:
                if v > 0:
                    e.wait_ge(semmap[s], v)
        self.cur[eng].append(w)

    def banks(self, lo, hi):
        return self.bank_res[lo:hi]

    def emit_pending(self):
        nc = self.nc
        for blkops in self.blocks[self.emitted:]:
            with nc.Block() as blk:
                if blkops["Q"]:
                    @blk.sync
                    def _(sync, ops=blkops["Q"]):
                        for f in ops:
                            f(sync)
                if blkops["T"]:
                    @blk.tensor
                    def _(tensor, ops=blkops["T"]):
                        for f in ops:
                            f(tensor)
                if blkops["S"]:
                    @blk.scalar
                    def _(scalar, ops=blkops["S"]):
                        for f in ops:
                            f(scalar)
                if blkops["V"]:
                    @blk.vector
                    def _(vector, ops=blkops["V"]):
                        for f in ops:
                            f(vector)
                if blkops["G"]:
                    @blk.gpsimd
                    def _(gpsimd, ops=blkops["G"]):
                        for f in ops:
                            f(gpsimd)
        self.emitted = len(self.blocks)


def build_program(dbg=None):
    dbg = dbg or {}
    nc = bass.Bass("TRN2", target_bir_lowering=False)
    ctx = ExitStack()

    x_dr = nc.dram_tensor("x", [2 * TOK, D], F32, kind="ExternalInput").ap()
    wqkv_dr = nc.dram_tensor("wqkv", [3 * D, D], F32, kind="ExternalInput").ap()
    wout_dr = nc.dram_tensor("wout", [D, D], F32, kind="ExternalInput").ap()
    k0_dr = nc.dram_tensor("k0", [2 * D, 4096], U8, kind="ExternalInput").ap()
    k1_dr = nc.dram_tensor("k1", [4096, 4096], U8, kind="ExternalInput").ap()
    k2_dr = nc.dram_tensor("k2", [4096, D], U8, kind="ExternalInput").ap()
    s_dr = nc.dram_tensor("s", [1, 4], F32, kind="ExternalInput").ap()
    out_dr = nc.dram_tensor("out", [TOK, D], F32, kind="ExternalOutput").ap()

    qscr = nc.dram_tensor("qscr", [TOK * 2, 512], BF16, kind="Internal").ap()
    w0scr = nc.dram_tensor("w0scr", [D, 4096], BF16, kind="Internal").ap()
    w1scr = nc.dram_tensor("w1scr", [4096, 4096], BF16, kind="Internal").ap()
    w2scr = nc.dram_tensor("w2scr", [4096, D], BF16, kind="Internal").ap()

    identf_dr = nc.inline_tensor(np.eye(128, dtype=np.float32), "identf_dr")
    identb_dr = nc.inline_tensor(
        np.eye(128, dtype=np.float32).astype(ml_dtypes.bfloat16), "identb_dr")
    mk = MK(nc, ctx)
    if dbg.get("serial"):
        mk.serial = Res("serial_token")
        mk.serial_engines = dbg.get("serial_engines")

    sb = lambda name, shape, dt: ctx.enter_context(nc.sbuf_tensor(name, shape, dt))
    identf = sb("identf", [128, 128], F32)
    identb = sb("identb", [128, 128], BF16)
    ones1 = sb("ones1", [1, 128], F32)
    s_sb = sb("s_sb", [1, 4], F32)
    sc = sb("sc", [128, 8], F32)
    cst = sb("cst", [128, 8], F32)
    woutT = sb("woutT", [128, 8 * D], BF16)
    kvblk = sb("kvblk", [128, 8 * 128], BF16)
    ksumblk = sb("ksumblk", [128, 16], BF16)
    stats = sb("stats", [128, 64], F32)
    sqscr = sb("sqscr", [128, 2048], BF16)
    rn = sb("rn", [128, 64], F32)

    C_SUM, C_SSQ, C_MEAN, C_VAR, C_R, C_B, C_T0 = 0, 1, 2, 3, 4, 5, 6

    R = lambda name: Res(name)
    r_s = R("s")
    r_sc = R("sc")
    r_cst = R("cst")
    r_ident = R("ident")
    r_woutT = R("woutT")
    r_kvblk = R("kvblk")
    r_ksumblk = R("ksumblk")
    r_stats = [R(f"stats{t}") for t in range(4)]
    r_sq = R("sq")
    r_rn = [R(f"rn{t}") for t in range(4)]
    r_qscr = [[R(f"qscr{st}_{oc}") for oc in range(8)] for st in range(NST2)]
    r_w0scr = [R(f"w0scr{k}") for k in range(8)]
    r_w1scr = [R(f"w1scr{k}") for k in range(32)]
    r_w2scr = [R(f"w2scr{k}") for k in range(32)]
    r_out = R("out")

    def ln_core(tc, src_ap, r_src, src_banks=(), pre_fn=None, n_feat=1024,
                extra_writes=()):
        st_ap = stats[:, tc * 16:(tc + 1) * 16]
        rs = r_stats[tc]
        er = list(src_banks)
        inv = 1.0 / n_feat

        def vfn(e):
            if pre_fn is not None:
                pre_fn(e)
            e.scalar_tensor_tensor(sqscr[:, 0:n_feat], src_ap, 1.0, src_ap,
                                   ALU.bypass, ALU.mult,
                                   accum_out=st_ap[:, C_SSQ:C_SSQ + 1])
            e.reduce_sum(st_ap[:, C_SUM:C_SUM + 1], src_ap,
                         axis=mybir.AxisListType.X, negate=True)
            e.drain()
            e.tensor_scalar_mul(st_ap[:, C_MEAN:C_MEAN + 1],
                                st_ap[:, C_SUM:C_SUM + 1], inv)
            e.drain()
            e.tensor_tensor(st_ap[:, C_T0:C_T0 + 1], st_ap[:, C_MEAN:C_MEAN + 1],
                            st_ap[:, C_MEAN:C_MEAN + 1], ALU.mult)
            e.drain()
            return e.scalar_tensor_tensor(
                st_ap[:, C_VAR:C_VAR + 1], st_ap[:, C_SSQ:C_SSQ + 1], inv,
                st_ap[:, C_T0:C_T0 + 1], ALU.mult, ALU.subtract)
        mk.group("V", vfn, reads=[r_src] + er,
                 writes=[rs] + list(extra_writes))

        def sfn(e):
            e.activation(st_ap[:, C_T0:C_T0 + 1], st_ap[:, C_VAR:C_VAR + 1],
                         AF.Ln, bias=cst[:, 0:1])
            e.activation(st_ap[:, C_R:C_R + 1], st_ap[:, C_T0:C_T0 + 1],
                         AF.Exp, scale=cst[:, 1:2])
            e.drain()
            e.activation(st_ap[:, C_B:C_B + 1], st_ap[:, C_MEAN:C_MEAN + 1],
                         AF.Identity, scale=st_ap[:, C_R:C_R + 1])
            return e.drain()
        mk.group("S", sfn, reads=[rs, r_cst], writes=[rs])
        return st_ap

    def ln_core_s(tc, src_ap, r_src, src_banks=(), pre_fn=None,
                  extra_writes=(), n_feat=1024, sum_on_v=False):
        st_ap = stats[:, tc * 16:(tc + 1) * 16]
        rs = r_stats[tc]
        er = list(src_banks)

        if sum_on_v:
            mk.group("V", lambda e: e.reduce_sum(
                st_ap[:, C_SUM:C_SUM + 1], src_ap,
                axis=mybir.AxisListType.X, negate=True),
                reads=[r_src] + er, writes=[rs])

        def sfn(e):
            if pre_fn is not None:
                pre_fn(e)
            e.activation(sqscr[:, 1024:1024 + n_feat], src_ap, AF.Square,
                         accum_out=st_ap[:, C_SSQ:C_SSQ + 1])
            if not sum_on_v:
                e.activation(sqscr[:, 1024:1024 + n_feat], src_ap, AF.Identity,
                             scale=cst[:, 2:3],
                             accum_out=st_ap[:, C_SUM:C_SUM + 1])
            e.drain()
            e.activation(st_ap[:, C_MEAN:C_MEAN + 1], st_ap[:, C_SUM:C_SUM + 1],
                         AF.Identity, scale=cst[:, 4:5])
            e.activation(st_ap[:, C_T0:C_T0 + 1], st_ap[:, C_MEAN:C_MEAN + 1],
                         AF.Square)
            e.activation(st_ap[:, C_VAR:C_VAR + 1], st_ap[:, C_T0:C_T0 + 1],
                         AF.Identity, scale=cst[:, 2:3])
            e.drain()
            e.activation(st_ap[:, C_T0:C_T0 + 1], st_ap[:, C_SSQ:C_SSQ + 1],
                         AF.Identity, scale=cst[:, 4:5],
                         bias=st_ap[:, C_VAR:C_VAR + 1])
            e.activation(st_ap[:, C_VAR:C_VAR + 1], st_ap[:, C_T0:C_T0 + 1],
                         AF.Ln, bias=cst[:, 0:1])
            e.activation(st_ap[:, C_R:C_R + 1], st_ap[:, C_VAR:C_VAR + 1],
                         AF.Exp, scale=cst[:, 1:2])
            e.drain()
            e.activation(st_ap[:, C_B:C_B + 1], st_ap[:, C_MEAN:C_MEAN + 1],
                         AF.Identity, scale=st_ap[:, C_R:C_R + 1])
            return e.drain()
        mk.group("S", sfn, reads=[r_src, r_cst] + er,
                 writes=[rs] + list(extra_writes))

    def ln_core_presum(tc, src_ap, r_src, n_feat=1024):
        st_ap = stats[:, tc * 16:(tc + 1) * 16]
        rs = r_stats[tc]
        inv = 1.0 / n_feat

        def vfn(e):
            e.scalar_tensor_tensor(sqscr[:, 0:n_feat], src_ap, 1.0, src_ap,
                                   ALU.bypass, ALU.mult,
                                   accum_out=st_ap[:, C_SSQ:C_SSQ + 1])
            e.drain()
            e.tensor_tensor(st_ap[:, C_MEAN:C_MEAN + 1],
                            st_ap[:, C_SUM:C_SUM + 1], cst[:, 5:6], ALU.mult)
            e.drain()
            e.tensor_tensor(st_ap[:, C_T0:C_T0 + 1], st_ap[:, C_MEAN:C_MEAN + 1],
                            st_ap[:, C_MEAN:C_MEAN + 1], ALU.mult)
            e.drain()
            return e.scalar_tensor_tensor(
                st_ap[:, C_VAR:C_VAR + 1], st_ap[:, C_SSQ:C_SSQ + 1], inv,
                st_ap[:, C_T0:C_T0 + 1], ALU.mult, ALU.subtract)
        mk.group("V", vfn, reads=[r_src, rs, r_cst], writes=[rs])

        def sfn(e):
            e.activation(st_ap[:, C_T0:C_T0 + 1], st_ap[:, C_VAR:C_VAR + 1],
                         AF.Ln, bias=cst[:, 0:1])
            e.activation(st_ap[:, C_R:C_R + 1], st_ap[:, C_T0:C_T0 + 1],
                         AF.Exp, scale=cst[:, 1:2])
            e.drain()
            e.activation(st_ap[:, C_B:C_B + 1], st_ap[:, C_MEAN:C_MEAN + 1],
                         AF.Identity, scale=st_ap[:, C_R:C_R + 1])
            return e.drain()
        mk.group("S", sfn, reads=[rs, r_cst], writes=[rs])

    def ln_any(tc, src_ap, r_src, on_s, src_banks=(), pre_fn=None,
               extra_writes=(), n_feat=1024):
        if on_s:
            ln_core_s(tc, src_ap, r_src, src_banks=src_banks, pre_fn=pre_fn,
                      extra_writes=extra_writes, n_feat=n_feat)
        else:
            ln_core(tc, src_ap, r_src, src_banks=src_banks, pre_fn=pre_fn,
                    extra_writes=extra_writes, n_feat=n_feat)

    def ln_apply_s(tc, dst_ap, src_ap, r_src, r_dst, src_banks=()):
        st_ap = stats[:, tc * 16:(tc + 1) * 16]
        mk.group("S", lambda e: e.activation(
            dst_ap, src_ap, AF.Identity,
            bias=st_ap[:, C_B:C_B + 1], scale=st_ap[:, C_R:C_R + 1]),
            reads=[r_src, r_stats[tc]] + list(src_banks), writes=[r_dst])

    def ln_apply_v(tc, dst_ap, src_ap, r_src, r_dst, src_banks=()):
        st_ap = stats[:, tc * 16:(tc + 1) * 16]
        mk.group("V", lambda e: e.tensor_scalar(
            dst_ap, src_ap, st_ap[:, C_MEAN:C_MEAN + 1],
            st_ap[:, C_R:C_R + 1], ALU.add, ALU.mult),
            reads=[r_src, r_stats[tc]] + list(src_banks), writes=[r_dst])

    def ln_apply_any(tc, dst_ap, src_ap, r_src, r_dst, on_v, src_banks=()):
        if on_v:
            ln_apply_v(tc, dst_ap, src_ap, r_src, r_dst, src_banks)
        else:
            ln_apply_s(tc, dst_ap, src_ap, r_src, r_dst, src_banks)

    tr_rr = [0]

    def transpose8(src_tile_ap, r_src, tr_ps, tr_bank_pair, dst_ap_fn, r_dst):
        g = tr_rr[0] % 2
        tr_rr[0] += 1
        bank = tr_bank_pair[g]
        def tfn(e, g=g):
            last = None
            for j in range(8):
                last = e.transpose(
                    tr_ps[:, g * 1024 + j * 128: g * 1024 + (j + 1) * 128],
                    src_tile_ap[:, j * 128:(j + 1) * 128], identb[:])
            return last
        mk.group("T", tfn, reads=[r_src, r_ident], writes=[bank])
        mk.group("V", lambda e, g=g: e.tensor_copy(
            dst_ap_fn(None),
            tr_ps[:, g * 1024:(g + 1) * 1024]
            .rearrange("p (j c) -> p j c", c=128)),
            reads=[bank], writes=[r_dst])

    p1 = ExitStack()
    p1sb = lambda name, shape, dt: p1.enter_context(nc.sbuf_tensor(name, shape, dt))
    wqkvT = p1sb("wqkvT", [128, 8 * 3072], BF16)
    wtmp = p1sb("wtmp", [128, 4096], F32)
    ku8 = p1sb("ku8", [128, 2 * 4096], U8)
    kb8 = p1sb("kb8", [128, 4096], U8)
    wcvt = p1sb("wcvt", [128, 2 * 4096], BF16)
    x1 = p1sb("x1", [128, 3 * 4096], F32)
    ln0f = p1sb("ln0f", [128, 2 * 4096], BF16)
    ln0T = p1sb("ln0T", [128, 4096], BF16)
    kvt = p1sb("kvt", [128, 1024], BF16)
    vext = p1sb("vext", [128, 1040], BF16)
    elu1 = p1sb("elu1", [128, 2 * 512], BF16)
    elu2 = p1sb("elu2", [128, 2 * 512], BF16)
    qtile = p1sb("qtile", [128, 4 * 512], BF16)

    ps1 = p1.enter_context(nc.psum_tensor("ps1", [128, 4096], F32))
    kv_ps = ps1[:, 0:1536]
    kv_banks = mk.banks(0, 3)
    proj_ps = ps1[:, 1536:3072]
    proj_banks = mk.banks(3, 6)
    tr_ps = ps1[:].bitcast(BF16)[:, 2 * 3072: 2 * 3072 + 2048]
    tr_banks = mk.banks(6, 8)

    r_wqkvT = R("wqkvT")
    r_wtmp = [R(f"wtmp{i}") for i in range(4)]
    r_ku8 = [R("ku80"), R("ku81")]
    r_kb8 = R("kb8")
    r_wcvt = [R("wcvt0"), R("wcvt1")]
    r_x1 = [R("x10"), R("x11"), R("x12")]
    r_ln0f = [[R(f"ln0f{p}_{t}") for t in range(4)] for p in range(2)]
    r_ln0T = R("ln0T")
    r_kvt = R("kvt")
    r_vext = R("vext")
    r_elu1 = [R("elu1a"), R("elu1b")]
    r_elu2 = [R("elu2a"), R("elu2b")]
    r_qtile = [R(f"qtile{i}") for i in range(4)]
    proj_rr = [0]
    elu_rr = [0]

    def kv_pair_col(j):
        return (j // 3) * 512 + (j % 3) * 130

    mk.new_block()
    mk.group("Q", lambda e, dma: (
        dma(identf[:], identf_dr.ap()[:, :]),
        dma(identb[:], identb_dr.ap()[:, :]),
        dma(s_sb[:], s_dr[:, :]),
    ), writes=[r_ident, r_s], dma_n=3)
    mk.group("V", lambda e: e.memset(ones1[:], 1.0), writes=[r_ident])
    early_x0 = [False]
    mk.group("T", lambda e: e.matmul(proj_ps[:, 0:4], ones1[:], s_sb[:],
                                     start=True, stop=True),
             reads=[r_s, r_ident], writes=[proj_banks[0]])
    mk.group("V", lambda e: e.tensor_scalar_mul(sc[:, 0:4], proj_ps[:, 0:4], 2.0),
             reads=[proj_banks[0]], writes=[r_sc])

    def init_fn(e):
        e.memset(kvblk[:], 0.0)
        e.memset(ksumblk[:], 0.0)
        e.memset(cst[:, 0:1], EPS)
        e.memset(cst[:, 1:2], -0.5)
        e.memset(cst[:, 2:3], -1.0)
        e.memset(cst[:, 3:4], 2.0)
        e.memset(cst[:, 4:5], 1.0 / 1024)
        e.memset(cst[:, 5:6], -1.0 / 1024)
        return e.memset(
            vext.ap().rearrange("p (h c) -> p h c", c=65)[:, :, 0:1], 1.0)
    mk.group("V", init_fn, writes=[r_kvblk, r_ksumblk, r_cst, r_vext])

    def prep_T(dr, n_oc, dstT, r_dst, dst_stride):
        for oc in range(n_oc):
            par = oc % 4
            mk.group("Q", lambda e, dma, oc=oc, par=par: dma(
                wtmp[:, par * 1024:(par + 1) * 1024],
                dr[oc * 128:(oc + 1) * 128, :]),
                reads=[r_wtmp[par]], writes=[r_wtmp[par]], dma_n=1)
            for grp in range(2):
                bank = proj_banks[grp]
                def tfn(e, oc=oc, par=par, grp=grp):
                    last = None
                    for i in range(4):
                        ii = grp * 4 + i
                        last = e.transpose(
                            proj_ps[:, grp * 512 + i * 128: grp * 512 + (i + 1) * 128],
                            wtmp[:, par * 1024 + ii * 128: par * 1024 + (ii + 1) * 128],
                            identf[:])
                    return last
                mk.group("T", tfn, reads=[r_wtmp[par], r_ident], writes=[bank])
                def efn(e, oc=oc, grp=grp, dstT=dstT, dst_stride=dst_stride):
                    return e.tensor_copy(
                        dstT.ap().rearrange("p (i c) -> p i c", c=dst_stride)
                        [:, grp * 4:(grp + 1) * 4, oc * 128:(oc + 1) * 128],
                        proj_ps[:, grp * 512:(grp + 1) * 512]
                        .rearrange("p (i c) -> p i c", c=128))
                mk.group("V", efn, reads=[bank], writes=[r_dst])

    def load_x1(st):
        par = st % 3
        mk.group("Q", lambda e, dma, st=st, par=par: [
            dma(x1[:, par * 4096 + tc * 1024: par * 4096 + (tc + 1) * 1024],
                x_dr[st * 512 + tc * 128: st * 512 + (tc + 1) * 128, :])
            for tc in range(4)][-1],
            reads=[r_x1[par]], writes=[r_x1[par]], dma_n=4)

    load_x1(0)
    prep_T(wqkv_dr, 24, wqkvT, r_wqkvT, 3072)
    prep_T(wout_dr, 8, woutT, r_woutT, 1024)

    prep_units = ([("w0", k) for k in range(8)]
                  + [("w1", k) for k in range(32)]
                  + [("w2", k) for k in range(32)])
    prep_pos = [0]

    def emit_prep_units(n):
        for _ in range(n):
            if prep_pos[0] >= len(prep_units):
                return
            which, k = prep_units[prep_pos[0]]
            prep_pos[0] += 1
            par = prep_pos[0] % 2
            if which == "w0":
                mk.group("Q", lambda e, dma, k=k, par=par: (
                    dma(ku8[:, par * 4096:(par + 1) * 4096],
                        k0_dr[k * 128:(k + 1) * 128, :]),
                    dma(kb8[:],
                        k0_dr[1024 + k * 128: 1024 + (k + 1) * 128, :]),
                ), reads=[r_ku8[par], r_kb8],
                    writes=[r_ku8[par], r_kb8], dma_n=2)
                mk.group("G", lambda e, par=par: e.tensor_tensor(
                    wcvt[:, par * 4096:(par + 1) * 4096],
                    ku8[:, par * 4096:(par + 1) * 4096], kb8[:], ALU.subtract),
                    reads=[r_ku8[par], r_kb8], writes=[r_wcvt[par]])
                mk.group("G", lambda e, dma, k=k, par=par: dma(
                    w0scr[k * 128:(k + 1) * 128, :],
                    wcvt[:, par * 4096:(par + 1) * 4096]),
                    reads=[r_wcvt[par]], writes=[r_w0scr[k]], dma_n=1)
                continue
            src, dst, rdst, w = (
                (k1_dr, w1scr, r_w1scr[k], 4096) if which == "w1"
                else (k2_dr, w2scr, r_w2scr[k], 1024))
            mk.group("Q", lambda e, dma, src=src, k=k, par=par, w=w: dma(
                ku8[:, par * 4096: par * 4096 + w],
                src[k * 128:(k + 1) * 128, :]),
                reads=[r_ku8[par]], writes=[r_ku8[par]], dma_n=1)
            mk.group("G", lambda e, par=par, w=w: e.tensor_scalar(
                wcvt[:, par * 4096: par * 4096 + w],
                ku8[:, par * 4096: par * 4096 + w], 2.0, -1.0,
                ALU.mult, ALU.add),
                reads=[r_ku8[par]], writes=[r_wcvt[par]])
            mk.group("G", lambda e, dma, dst=dst, k=k, par=par, w=w: dma(
                dst[k * 128:(k + 1) * 128, :],
                wcvt[:, par * 4096: par * 4096 + w]),
                reads=[r_wcvt[par]], writes=[rdst], dma_n=1)

    def ln_chain1(st, tc):
        par = st % 2
        xpar = st % 3
        xs = x1[:, xpar * 4096 + tc * 1024: xpar * 4096 + (tc + 1) * 1024]
        ln_any(tc, xs, r_x1[xpar], on_s=(tc % 2 == 0))
        ln_apply_s(tc, ln0f[:, par * 4096 + tc * 1024:
                           par * 4096 + (tc + 1) * 1024], xs,
                   r_x1[xpar], r_ln0f[par][tc])

    def ln_chains1(st):
        for tc in range(4):
            ln_chain1(st, tc)

    first_kv = [True]
    ln_chains1(0)
    for st in range(NST1):
        par = st % 2
        own = st < NST2

        if st + 1 < NST1:
            load_x1(st + 1)
        emit_prep_units(5 if st > 0 else 0)

        for tc in range(4):
            transpose8(
                ln0f[:, par * 4096 + tc * 1024: par * 4096 + (tc + 1) * 1024],
                r_ln0f[par][tc], tr_ps, tr_banks,
                lambda g, tc=tc: ln0T.ap()
                .rearrange("p (j c) -> p j c", c=512)
                [:, :, tc * 128:(tc + 1) * 128],
                r_ln0T)

            for ot in range(4):
                pj = proj_rr[0] % 3
                proj_rr[0] += 1
                pb = proj_banks[pj]
                pslice = proj_ps[:, pj * 512:(pj + 1) * 512]
                def mmfn(e, tc=tc, ot=ot, pslice=pslice):
                    last = None
                    for j in range(8):
                        last = e.matmul(
                            pslice,
                            ln0T[:, j * 512 + tc * 128: j * 512 + (tc + 1) * 128],
                            wqkvT[:, j * 3072 + 1024 + ot * 512:
                                  j * 3072 + 1024 + (ot + 1) * 512],
                            start=(j == 0), stop=(j == 7))
                    return last
                mk.group("T", mmfn, reads=[r_ln0T, r_wqkvT], writes=[pb])
                if ot < 2:
                    ep = elu_rr[0] % 2
                    elu_rr[0] += 1
                    e1 = elu1[:, ep * 512:(ep + 1) * 512]
                    e2 = elu2[:, ep * 512:(ep + 1) * 512]
                    mk.group("V", lambda e, pslice=pslice, e1=e1: e.tensor_scalar(
                        e1, pslice, 0.0, None, ALU.min),
                        reads=[pb], writes=[r_elu1[ep]])
                    mk.group("S", lambda e, e1=e1, e2=e2: e.activation(
                        e2, e1, AF.Exp),
                        reads=[r_elu1[ep]], writes=[r_elu2[ep]])
                    mk.group("V", lambda e, ot=ot, pslice=pslice, e2=e2:
                             e.scalar_tensor_tensor(
                                 kvt[:, ot * 512:(ot + 1) * 512], pslice, 0.0,
                                 e2, ALU.max, ALU.add),
                             reads=[pb, r_elu2[ep]], writes=[r_kvt])
                else:
                    hbase = (ot - 2) * 8
                    mk.group("V", lambda e, hbase=hbase, pslice=pslice:
                             e.tensor_copy(
                                 vext.ap().rearrange("p (h c) -> p h c", c=65)
                                 [:, hbase:hbase + 8, 1:65],
                                 pslice.rearrange("p (h c) -> p h c", c=64)),
                             reads=[pb], writes=[r_vext])
            is_last_kv = (st == NST1 - 1 and tc == 3)
            def kvfn(e, first=first_kv[0], last_kv=is_last_kv):
                last = None
                for j in range(8):
                    last = e.matmul(
                        kv_ps[:, kv_pair_col(j): kv_pair_col(j) + 130],
                        kvt[:, j * 128:(j + 1) * 128],
                        vext[:, j * 130:(j + 1) * 130],
                        start=(first and j % 3 == 0), stop=last_kv)
                return last
            mk.group("T", kvfn, reads=[r_kvt, r_vext], writes=list(kv_banks))
            first_kv[0] = False
            if st + 1 < NST1:
                ln_chain1(st + 1, tc)

        if own:
            stq = st
            for oc in range(8):
                pj = proj_rr[0] % 3
                proj_rr[0] += 1
                pb = proj_banks[pj]
                pslice = proj_ps[:, pj * 512:(pj + 1) * 512]
                qpar = oc % 4
                def qmm(e, oc=oc, pslice=pslice):
                    last = None
                    for j in range(8):
                        last = e.matmul(
                            pslice,
                            wqkvT[:, j * 3072 + oc * 128: j * 3072 + (oc + 1) * 128],
                            ln0T[:, j * 512:(j + 1) * 512],
                            start=(j == 0), stop=(j == 7))
                    return last
                mk.group("T", qmm, reads=[r_ln0T, r_wqkvT], writes=[pb])
                ep = elu_rr[0] % 2
                elu_rr[0] += 1
                e1 = elu1[:, ep * 512:(ep + 1) * 512]
                e2 = elu2[:, ep * 512:(ep + 1) * 512]
                mk.group("V", lambda e, pslice=pslice, e1=e1: e.tensor_scalar(
                    e1, pslice, 0.0, None, ALU.min),
                    reads=[pb], writes=[r_elu1[ep]])
                mk.group("S", lambda e, e1=e1, e2=e2: e.activation(e2, e1, AF.Exp),
                         reads=[r_elu1[ep]], writes=[r_elu2[ep]])
                mk.group("V", lambda e, qpar=qpar, pslice=pslice, e2=e2:
                         e.scalar_tensor_tensor(
                             qtile[:, qpar * 512:(qpar + 1) * 512], pslice, 0.0,
                             e2, ALU.max, ALU.add),
                         reads=[pb, r_elu2[ep]], writes=[r_qtile[qpar]])
                mk.group("Q", lambda e, dma, stq=stq, oc=oc, qpar=qpar: dma(
                    qscr[(stq * 8 + oc) * 128:(stq * 8 + oc + 1) * 128, :],
                    qtile[:, qpar * 512:(qpar + 1) * 512]),
                    reads=[r_qtile[qpar]], writes=[r_qscr[stq][oc]], dma_n=1)

    emit_prep_units(len(prep_units))

    def kvevict(e):
        last = None
        for j in range(8):
            c = kv_pair_col(j)
            last = e.tensor_copy(kvblk[0:64, j * 128: j * 128 + 64],
                                 kv_ps[0:64, c + 1: c + 65])
            last = e.tensor_copy(kvblk[64:128, j * 128 + 64: j * 128 + 128],
                                 kv_ps[64:128, c + 66: c + 130])
            last = e.tensor_copy(ksumblk[0:64, 2 * j: 2 * j + 1],
                                 kv_ps[0:64, c: c + 1])
            last = e.tensor_copy(ksumblk[64:128, 2 * j + 1: 2 * j + 2],
                                 kv_ps[64:128, c + 65: c + 66])
        return last
    mk.group("V", kvevict, reads=list(kv_banks), writes=[r_kvblk, r_ksumblk])

    if dbg is not None and "phase1_end" in dbg:
        dbg["phase1_end"](nc, mk, locals())

    mk.wait_all("Q", "T")
    mk.wait_all("Q", "V")
    mk.wait_all("Q", "S")
    mk.wait_all("Q", "Qd")
    mk.wait_all("Q", "Sd")

    mk.emit_pending()
    p1.close()
    mk.bank_res = [Res(f"bank{i}") for i in range(8)]

    p2 = ExitStack()
    p2sb = lambda name, shape, dt: p2.enter_context(nc.sbuf_tensor(name, shape, dt))
    x2 = p2sb("x2", [128, 2 * 4096], F32)
    q2 = p2sb("q2", [128, 2 * 4096], BF16)
    atm = p2sb("atm", [128, 2 * 1024], BF16)
    afm = p2sb("afm", [128, 4096], BF16)
    tln = p2sb("tln", [128, 3 * 1024], BF16)
    ln2T = p2sb("ln2T", [128, 4096], BF16)
    h0 = p2sb("h0", [128, 16384], BF16)
    h1 = p2sb("h1", [128, 16384], BF16)
    h2b = p2sb("h2b", [128, 4 * 1024], F32)
    outb = p2sb("outb", [128, 2 * 1024], F32)
    ybuf = p2sb("ybuf", [128, 2 * 1024], F32)
    wring = p2sb("wring", [128, 3 * 2048], BF16)

    ps2 = p2.enter_context(nc.psum_tensor("ps2", [128, 4096], F32))
    att_psp = [ps2[:, 0:1024], ps2[:, 1024:2048]]
    att_banksp = [mk.banks(0, 2), mk.banks(2, 4)]
    y_psp = att_psp
    y_banksp = att_banksp
    norm_psp = [ps2[:, 2048:2560], ps2[:, 3584:4096]]
    norm_banksp = [mk.banks(4, 5), mk.banks(7, 8)]
    trp_ps = ps2[:].bitcast(BF16)[:, 2 * 2560: 2 * 2560 + 2048]
    trp_banks = mk.banks(5, 7)
    ffn_ps = ps2
    ffn_banks = mk.banks(0, 8)

    r_x2 = [R("x20"), R("x21")]
    r_q2 = [R("q20"), R("q21")]
    r_atm = [R("atm0"), R("atm1")]
    r_afm = [R(f"afm{t}") for t in range(4)]
    r_tln = [R(f"tln{i}") for i in range(3)]
    r_ln2T = R("ln2T")
    r_h0 = [R("h0a"), R("h0b")]
    r_h1 = [R("h1a"), R("h1b")]
    r_h2b = [R(f"h2b{i}") for i in range(4)]
    r_outb = [R("outb0"), R("outb1")]
    r_ybuf = [R("ybuf0"), R("ybuf1")]
    r_wring = [R(f"wring{i}") for i in range(3)]

    mk.new_block()

    def load_xq(st):
        par = st % 2
        mk.group("Q", lambda e, dma, st=st, par=par: [
            dma(x2[:, par * 4096 + tc * 1024: par * 4096 + (tc + 1) * 1024],
                x_dr[st * 512 + tc * 128: st * 512 + (tc + 1) * 128, :])
            for tc in range(4)][-1],
            reads=[r_x2[par]], writes=[r_x2[par]], dma_n=4)
        mk.group("Q", lambda e, dma, st=st, par=par: [
            dma(q2[:, par * 4096 + j * 512: par * 4096 + (j + 1) * 512],
                qscr[(st * 8 + j) * 128:(st * 8 + j + 1) * 128, :])
            for j in range(8)][-1],
            reads=[r_q2[par]] + r_qscr[st], writes=[r_q2[par]], dma_n=8)

    load_xq(0)
    pending_tail = []

    for st in range(NST2):
        par = st % 2
        def attT_emit(tc):
            tpar = tc % 2
            transpose8(
                atm[:, tpar * 1024:(tpar + 1) * 1024], r_atm[tpar],
                trp_ps, trp_banks,
                lambda g, tc=tc: afm.ap()
                .rearrange("p (j c) -> p j c", c=512)
                [:, :, tc * 128:(tc + 1) * 128],
                r_afm[tc])

        def ln2T_emit(tc):
            tsl = tc % 3
            transpose8(
                tln[:, tsl * 1024:(tsl + 1) * 1024], r_tln[tsl],
                trp_ps, trp_banks,
                lambda g, tc=tc: ln2T.ap()
                .rearrange("p (j c) -> p j c", c=512)
                [:, :, tc * 128:(tc + 1) * 128],
                r_ln2T)

        for tc in range(4):
            tpar = tc % 2
            norm_ps = norm_psp[tpar]
            norm_banks = norm_banksp[tpar]
            def nfn(e, tc=tc, par=par, norm_ps=norm_ps):
                last = None
                for j in range(8):
                    last = e.matmul(
                        norm_ps[:, tc * 16 + 2 * j: tc * 16 + 2 * j + 2],
                        q2[:, par * 4096 + j * 512 + tc * 128:
                           par * 4096 + j * 512 + (tc + 1) * 128],
                        ksumblk[:, 2 * j: 2 * j + 2],
                        start=True, stop=True)
                return last
            mk.group("T", nfn, reads=[r_q2[par], r_ksumblk], writes=list(norm_banks))
            att_ps = att_psp[tpar]
            def afn(e, tc=tc, par=par, att_ps=att_ps):
                last = None
                for j in range(8):
                    last = e.matmul(
                        att_ps[:, j * 128:(j + 1) * 128],
                        q2[:, par * 4096 + j * 512 + tc * 128:
                           par * 4096 + j * 512 + (tc + 1) * 128],
                        kvblk[:, j * 128:(j + 1) * 128],
                        start=True, stop=True)
                return last
            mk.group("T", afn, reads=[r_q2[par], r_kvblk],
                     writes=list(att_banksp[tpar]))
            def rnfn(e, tc=tc, norm_ps=norm_ps):
                e.tensor_scalar(rn[:, tc * 16:(tc + 1) * 16],
                                norm_ps[:, tc * 16:(tc + 1) * 16],
                                1e-6, None, ALU.max)
                e.drain()
                last = e.reciprocal(rn[:, tc * 16:(tc + 1) * 16],
                                    rn[:, tc * 16:(tc + 1) * 16])
                e.drain()
                return last
            mk.group("V", rnfn, reads=list(norm_banks), writes=[r_rn[tc]])
            mk.group("V", lambda e, tc=tc, tpar=tpar, att_ps=att_ps:
                e.tensor_tensor(
                    atm[:, tpar * 1024:(tpar + 1) * 1024]
                    .rearrange("p (h c) -> p h c", c=64),
                    att_ps.rearrange("p (h c) -> p h c", c=64),
                    rn[:, tc * 16:(tc + 1) * 16].unsqueeze(2)
                    .broadcast_to([128, 16, 64]),
                    ALU.mult),
                reads=list(att_banksp[tpar]) + [r_rn[tc]], writes=[r_atm[tpar]])
            if tc >= 1:
                attT_emit(tc - 1)
        attT_emit(2)
        attT_emit(3)

        for fn_ in pending_tail:
            fn_()
        pending_tail.clear()

        if st + 1 < NST2:
            load_xq(st + 1)

        for tc in range(4):
            tpar = tc % 2
            y_ps = y_psp[tpar]
            def yfn(e, tc=tc, y_ps=y_ps):
                last = None
                for oh in range(2):
                    for j in range(8):
                        last = e.matmul(
                            y_ps[:, oh * 512:(oh + 1) * 512],
                            afm[:, j * 512 + tc * 128: j * 512 + (tc + 1) * 128],
                            woutT[:, j * 1024 + oh * 512: j * 1024 + (oh + 1) * 512],
                            start=(j == 0), stop=(j == 7))
                return last
            mk.group("T", yfn, reads=[r_afm[tc], r_woutT],
                     writes=list(y_banksp[tpar]))
            yb = ybuf[:, tpar * 1024:(tpar + 1) * 1024]
            st_ap = stats[:, tc * 16:(tc + 1) * 16]
            xs = x2[:, par * 4096 + tc * 1024: par * 4096 + (tc + 1) * 1024]
            mk.group("S", lambda e, yb=yb, y_ps=y_ps, st_ap=st_ap:
                     e.activation(yb, y_ps, AF.Identity,
                                  accum_out=st_ap[:, C_SUM:C_SUM + 1]),
                     reads=list(y_banksp[tpar]),
                     writes=[r_ybuf[tpar], r_stats[tc]])
            ln_core_presum(tc, yb, r_ybuf[tpar])
            def add1(e, xs=xs, yb=yb, st_ap=st_ap):
                e.tensor_scalar(yb, yb, st_ap[:, C_MEAN:C_MEAN + 1],
                                st_ap[:, C_R:C_R + 1], ALU.add, ALU.mult)
                return e.tensor_add(xs, xs, yb)
            mk.group("V", add1,
                     reads=[r_ybuf[tpar], r_stats[tc], r_x2[par]],
                     writes=[r_x2[par], r_ybuf[tpar]])
            ln_any(tc, xs, r_x2[par], on_s=(tc % 2 == 1))
            tsl = tc % 3
            ln_apply_s(tc, tln[:, tsl * 1024:(tsl + 1) * 1024], xs,
                       r_x2[par], r_tln[tsl])
            if tc >= 2:
                ln2T_emit(tc - 2)
        ln2T_emit(2)
        ln2T_emit(3)

        def ffn_layer(nk, wsrc_scr, r_wscr, rhs_sb, r_rhs, dst_sb, r_dst,
                      scale_col, relu, wq):
            for ocb in range(4):
                for kp in range(nk // 2):
                    ring = (ocb * (nk // 2) + kp) % 3
                    k0_ = 2 * kp
                    mk.group(wq, lambda e, dma, k0_=k0_, ocb=ocb, ring=ring,
                             wsrc_scr=wsrc_scr: dma(
                                 wring[:, ring * 2048:(ring + 1) * 2048]
                                 .rearrange("p (a c) -> p a c", c=1024),
                                 wsrc_scr[k0_ * 128:(k0_ + 2) * 128,
                                          ocb * 1024:(ocb + 1) * 1024]
                                 .rearrange("(a r) c -> r a c", r=128)),
                             reads=[r_wring[ring], r_wscr[k0_], r_wscr[k0_ + 1]],
                             writes=[r_wring[ring]], dma_n=1)
                    for bh in range(2):
                        def mmfn(e, k0_=k0_, ring=ring, nk=nk, rhs_sb=rhs_sb,
                                 bh=bh):
                            last = None
                            for half in range(2):
                                k = k0_ + half
                                for oc in range(bh * 4, bh * 4 + 4):
                                    last = e.matmul(
                                        ffn_ps[:, oc * 512:(oc + 1) * 512],
                                        wring[:, ring * 2048 + half * 1024 + oc * 128:
                                              ring * 2048 + half * 1024 + (oc + 1) * 128],
                                        rhs_sb[:, k * 512:(k + 1) * 512],
                                        start=(k == 0), stop=(k == nk - 1))
                            return last
                        mk.group("T", mmfn,
                                 reads=[r_wring[ring]] + list(r_rhs),
                                 writes=list(ffn_banks[bh * 4:bh * 4 + 4]))
                def evfnS(e, ocb=ocb, dst_sb=dst_sb, scale_col=scale_col,
                          relu=relu):
                    last = None
                    for oc in range(4):
                        last = e.activation(
                            dst_sb[:, (ocb * 8 + oc) * 512:
                                   (ocb * 8 + oc + 1) * 512],
                            ffn_ps[:, oc * 512:(oc + 1) * 512],
                            AF.Relu if relu else AF.Copy,
                            scale=sc[:, scale_col:scale_col + 1])
                    return last
                mk.group("S", evfnS, reads=list(ffn_banks[0:4]) + [r_sc],
                         writes=[r_dst[0]])
                def evfnV(e, ocb=ocb, dst_sb=dst_sb, scale_col=scale_col,
                          relu=relu):
                    last = None
                    for oc in range(4, 8):
                        if relu:
                            last = e.tensor_scalar(
                                dst_sb[:, (ocb * 8 + oc) * 512:
                                       (ocb * 8 + oc + 1) * 512],
                                ffn_ps[:, oc * 512:(oc + 1) * 512],
                                sc[:, scale_col:scale_col + 1], 0.0,
                                ALU.mult, ALU.max)
                        else:
                            last = e.tensor_scalar_mul(
                                dst_sb[:, (ocb * 8 + oc) * 512:
                                       (ocb * 8 + oc + 1) * 512],
                                ffn_ps[:, oc * 512:(oc + 1) * 512],
                                sc[:, scale_col:scale_col + 1])
                    return last
                mk.group("V", evfnV, reads=list(ffn_banks[4:8]) + [r_sc],
                         writes=[r_dst[1]])

        ffn_layer(8, w0scr, r_w0scr, ln2T, [r_ln2T], h0, r_h0, 0, True, "G")
        ffn_layer(32, w1scr, r_w1scr, h0, r_h0, h1, r_h1, 1, True, "Q")

        for kp in range(16):
            ring = kp % 3
            k0_ = 2 * kp
            mk.group("G", lambda e, dma, k0_=k0_, ring=ring: dma(
                wring[:, ring * 2048:(ring + 1) * 2048]
                .rearrange("p (a c) -> p a c", c=1024),
                w2scr[k0_ * 128:(k0_ + 2) * 128, :]
                .rearrange("(a r) c -> r a c", r=128)),
                reads=[r_wring[ring], r_w2scr[k0_], r_w2scr[k0_ + 1]],
                writes=[r_wring[ring]], dma_n=1)
            def h2fn(e, k0_=k0_, ring=ring):
                last = None
                for half in range(2):
                    k = k0_ + half
                    for tc in range(4):
                        for oh in range(2):
                            last = e.matmul(
                                ffn_ps[:, (tc * 2 + oh) * 512:(tc * 2 + oh + 1) * 512],
                                h1[:, k * 512 + tc * 128: k * 512 + (tc + 1) * 128],
                                wring[:, ring * 2048 + half * 1024 + oh * 512:
                                      ring * 2048 + half * 1024 + (oh + 1) * 512],
                                start=(k == 0), stop=(k == 31))
                return last
            mk.group("T", h2fn, reads=[r_wring[ring]] + r_h1,
                     writes=list(ffn_banks))
        def h2ev_s(e):
            e.activation(h2b[:, 0:1024], ffn_ps[:, 0:1024], AF.Copy,
                         scale=sc[:, 2:3])
            return e.activation(h2b[:, 1024:2048], ffn_ps[:, 1024:2048],
                                AF.Copy, scale=sc[:, 2:3])
        mk.group("S", h2ev_s, reads=list(ffn_banks[0:4]) + [r_sc],
                 writes=[r_h2b[0], r_h2b[1]])
        def h2ev_v(e):
            e.tensor_scalar_mul(h2b[:, 2048:3072], ffn_ps[:, 2048:3072],
                                sc[:, 2:3])
            return e.tensor_scalar_mul(h2b[:, 3072:4096], ffn_ps[:, 3072:4096],
                                       sc[:, 2:3])
        mk.group("V", h2ev_v, reads=list(ffn_banks[4:8]) + [r_sc],
                 writes=[r_h2b[2], r_h2b[3]])
        for tc in range(4):
            h2s = h2b[:, tc * 1024:(tc + 1) * 1024]
            ln_core_s(tc, h2s, r_h2b[tc])
            st_ap = stats[:, tc * 16:(tc + 1) * 16]
            xs = x2[:, par * 4096 + tc * 1024: par * 4096 + (tc + 1) * 1024]
            def emit_add3(tc=tc, st=st, par=par, h2s=h2s, st_ap=st_ap, xs=xs):
                tpar = tc % 2
                def add3(e, xs=xs, st_ap=st_ap, tpar=tpar, h2s=h2s):
                    e.tensor_scalar_add(xs, xs, st_ap[:, C_B:C_B + 1])
                    return e.scalar_tensor_tensor(
                        outb[:, tpar * 1024:(tpar + 1) * 1024], h2s,
                        st_ap[:, C_R:C_R + 1], xs, ALU.mult, ALU.add)
                mk.group("V", add3,
                         reads=[r_h2b[tc], r_stats[tc], r_x2[par]],
                         writes=[r_x2[par], r_outb[tpar]])
                mk.group("Q", lambda e, dma, st=st, tc=tc, tpar=tpar: dma(
                    out_dr[st * 512 + tc * 128: st * 512 + (tc + 1) * 128, :],
                    outb[:, tpar * 1024:(tpar + 1) * 1024]),
                    reads=[r_outb[tpar]], dma_n=1)
            pending_tail.append(emit_add3)

    for fn_ in pending_tail:
        fn_()
    pending_tail.clear()

    if dbg is not None and "phase2_end" in dbg:
        dbg["phase2_end"](nc, mk, locals())

    mk.wait_all("Q", "Qd")
    mk.wait_all("Q", "Sd")

    mk.emit_pending()
    p2.close()
    ctx.close()
    return nc


_cache = {}


def kernel(L0_attn_x, L0_attn_w_qkv, L0_attn_w_out, L0_ffn_k0, L0_ffn_k1,
           L0_ffn_k2, L0_ffn_s0, L0_ffn_s1, L0_ffn_s2):
    x = np.ascontiguousarray(np.asarray(L0_attn_x, dtype=np.float32))
    wqkv = np.ascontiguousarray(np.asarray(L0_attn_w_qkv, dtype=np.float32))
    wout = np.ascontiguousarray(np.asarray(L0_attn_w_out, dtype=np.float32))
    k0 = np.ascontiguousarray(np.asarray(L0_ffn_k0)).view(np.uint8)
    k1 = np.ascontiguousarray(np.asarray(L0_ffn_k1)).view(np.uint8)
    k2 = np.ascontiguousarray(np.asarray(L0_ffn_k2)).view(np.uint8)
    s = np.zeros((1, 4), np.float32)
    s[0, 0] = np.asarray(L0_ffn_s0)
    s[0, 1] = np.asarray(L0_ffn_s1)
    s[0, 2] = np.asarray(L0_ffn_s2)

    if "nc" not in _cache:
        _cache["nc"] = build_program()
    nc = _cache["nc"]

    in_maps = []
    for c in range(8):
        b, half = c // 2, c % 2
        own = x[b, half * TOK:(half + 1) * TOK]
        other = x[b, (1 - half) * TOK:(2 - half) * TOK]
        xc = np.concatenate([own, other], axis=0)
        in_maps.append({"x": xc, "wqkv": wqkv, "wout": wout,
                        "k0": k0, "k1": k1, "k2": k2, "s": s})
    res = run_bass_kernel_spmd(nc, in_maps, core_ids=list(range(8)))
    out = np.empty((B, S, D), np.float32)
    for c in range(8):
        b, half = c // 2, c % 2
        out[b, half * TOK:(half + 1) * TOK] = res.results[c]["out"]
    return out


# version 37 (final)
# speedup vs baseline: 1.0545x; 1.0545x over previous
import numpy as np
import ml_dtypes
import concourse.bass as bass
import concourse.mybir as mybir
from concourse.bass_utils import run_bass_kernel_spmd
from contextlib import ExitStack

F32 = mybir.dt.float32
BF16 = mybir.dt.bfloat16
U8 = mybir.dt.uint8
AF = mybir.ActivationFunctionType
ALU = mybir.AluOpType

B, S, D, H, HD = 4, 8192, 1024, 16, 64
TOK = 4096
NST1 = 16
NST2 = 8
EPS = 1e-5


class Res:
    __slots__ = ("name", "writer", "readers")

    def __init__(self, name):
        self.name = name
        self.writer = None
        self.readers = []


class MK:

    NLANES = 8

    def __init__(self, nc, ctx):
        self.nc = nc
        self.blocks = []
        self.cur = None
        self.cnt = {"T": 0, "V": 0, "S": 0, "G": 0}
        self.sem = {k: ctx.enter_context(nc.semaphore("sem" + k)) for k in self.cnt}
        for q in ("Q", "S", "G"):
            for i in range(self.NLANES):
                name = f"{q}L{i}"
                self.sem[name] = ctx.enter_context(nc.semaphore(name))
                self.cnt[name] = 0
        self.lane_rr = {"Q": 0, "S": 0, "G": 0}
        self.waited = {}
        self.bank_res = [Res(f"bank{i}") for i in range(8)]
        self.emitted = 0
        self.uid = 0
        self.serial = None

    def new_block(self):
        self.cur = {"T": [], "V": [], "S": [], "Q": [], "G": []}
        self.blocks.append(self.cur)
        self.waited = {}

    _ENG_SEM = {"T": "T", "V": "V", "S": "S", "G": "G"}

    serial_engines = None

    def group(self, eng, fn, reads=(), writes=(), dma_n=0):
        if self.serial is not None and (
                self.serial_engines is None
                or (not dma_n and eng in self.serial_engines)
                or (dma_n and ("Qd" if eng == "Q" else "Sd") in self.serial_engines)):
            reads = list(reads)
            writes = list(writes) + [self.serial]
        waits = {}

        def need(wv):
            if wv is None:
                return
            for sname, val in wv:
                if not dma_n and sname == self._ENG_SEM.get(eng):
                    continue
                key = (eng, sname)
                if self.waited.get(key, 0) >= val:
                    continue
                if waits.get(sname, 0) < val:
                    waits[sname] = val

        for r in reads:
            need(r.writer)
        for w in writes:
            need(w.writer)
            for rd in w.readers:
                need(rd)

        if dma_n:
            qname = eng if eng in ("Q", "G") else "S"
            lane_plan = []
            for _ in range(dma_n):
                li = self.lane_rr[qname]
                self.lane_rr[qname] = (li + 1) % self.NLANES
                lname = f"{qname}L{li}"
                prev = self.cnt[lname]
                self.cnt[lname] = prev + 16
                lane_plan.append((lname, prev, prev + 16))
            tag = [(ln, tv) for ln, pv, tv in lane_plan]
        else:
            semname = self._ENG_SEM[eng]
            self.cnt[semname] += 1
            tag = [(semname, self.cnt[semname])]

        wl = sorted(waits.items())
        for sname, val in wl:
            self.waited[(eng, sname)] = max(self.waited.get((eng, sname), 0), val)
        semmap = self.sem

        if dma_n:
            def closure(e, wl=wl, fn=fn, lane_plan=lane_plan):
                for sname, val in wl:
                    e.wait_ge(semmap[sname], val)
                k = [0]
                def dma(out_ap, in_ap):
                    lname, pv, tv = lane_plan[k[0]]
                    k[0] += 1
                    if pv > 0:
                        e.wait_ge(semmap[lname], pv)
                    e.dma_start(out_ap, in_ap).then_inc(semmap[lname], 16)
                fn(e, dma)
                assert k[0] == len(lane_plan)
        else:
            inc_sem = semmap[self._ENG_SEM[eng]]
            def closure(e, wl=wl, fn=fn, inc_sem=inc_sem):
                for sname, val in wl:
                    e.wait_ge(semmap[sname], val)
                fn(e).then_inc(inc_sem, 1)

        self.cur[eng].append(closure)
        for r in reads:
            r.readers.append(tag)
        for w in writes:
            w.writer = tag
            w.readers = []
        return tag

    def wait_all(self, eng, what):
        if what in ("Qd", "Sd", "Gd"):
            q = what[0]
            pairs = [(f"{q}L{i}", self.cnt[f"{q}L{i}"])
                     for i in range(self.NLANES)]
        else:
            pairs = [(what, self.cnt[what])]
        semmap = self.sem

        def w(e, pairs=pairs):
            for s, v in pairs:
                if v > 0:
                    e.wait_ge(semmap[s], v)
        self.cur[eng].append(w)

    def banks(self, lo, hi):
        return self.bank_res[lo:hi]

    def emit_pending(self):
        nc = self.nc
        for blkops in self.blocks[self.emitted:]:
            with nc.Block() as blk:
                if blkops["Q"]:
                    @blk.sync
                    def _(sync, ops=blkops["Q"]):
                        for f in ops:
                            f(sync)
                if blkops["T"]:
                    @blk.tensor
                    def _(tensor, ops=blkops["T"]):
                        for f in ops:
                            f(tensor)
                if blkops["S"]:
                    @blk.scalar
                    def _(scalar, ops=blkops["S"]):
                        for f in ops:
                            f(scalar)
                if blkops["V"]:
                    @blk.vector
                    def _(vector, ops=blkops["V"]):
                        for f in ops:
                            f(vector)
                if blkops["G"]:
                    @blk.gpsimd
                    def _(gpsimd, ops=blkops["G"]):
                        for f in ops:
                            f(gpsimd)
        self.emitted = len(self.blocks)


def build_program(dbg=None):
    dbg = dbg or {}
    nc = bass.Bass("TRN2", target_bir_lowering=False)
    ctx = ExitStack()

    x_dr = nc.dram_tensor("x", [2 * TOK, D], F32, kind="ExternalInput").ap()
    wqkv_dr = nc.dram_tensor("wqkv", [3 * D, D], F32, kind="ExternalInput").ap()
    wout_dr = nc.dram_tensor("wout", [D, D], F32, kind="ExternalInput").ap()
    k0_dr = nc.dram_tensor("k0", [2 * D, 4096], U8, kind="ExternalInput").ap()
    k1_dr = nc.dram_tensor("k1", [4096, 4096], U8, kind="ExternalInput").ap()
    k2_dr = nc.dram_tensor("k2", [4096, D], U8, kind="ExternalInput").ap()
    s_dr = nc.dram_tensor("s", [1, 4], F32, kind="ExternalInput").ap()
    out_dr = nc.dram_tensor("out", [TOK, D], F32, kind="ExternalOutput").ap()

    qscr = nc.dram_tensor("qscr", [TOK * 2, 512], BF16, kind="Internal").ap()
    w0scr = nc.dram_tensor("w0scr", [D, 4096], BF16, kind="Internal").ap()
    w1scr = nc.dram_tensor("w1scr", [4096, 4096], BF16, kind="Internal").ap()
    w2scr = nc.dram_tensor("w2scr", [4096, D], BF16, kind="Internal").ap()

    identf_dr = nc.inline_tensor(np.eye(128, dtype=np.float32), "identf_dr")
    identb_dr = nc.inline_tensor(
        np.eye(128, dtype=np.float32).astype(ml_dtypes.bfloat16), "identb_dr")
    mk = MK(nc, ctx)
    if dbg.get("serial"):
        mk.serial = Res("serial_token")
        mk.serial_engines = dbg.get("serial_engines")

    sb = lambda name, shape, dt: ctx.enter_context(nc.sbuf_tensor(name, shape, dt))
    identf = sb("identf", [128, 128], F32)
    identb = sb("identb", [128, 128], BF16)
    ones1 = sb("ones1", [1, 128], F32)
    s_sb = sb("s_sb", [1, 4], F32)
    sc = sb("sc", [128, 8], F32)
    cst = sb("cst", [128, 8], F32)
    woutT = sb("woutT", [128, 8 * D], BF16)
    kvblk = sb("kvblk", [128, 8 * 128], BF16)
    ksumblk = sb("ksumblk", [128, 16], BF16)
    stats = sb("stats", [128, 64], F32)
    sqscr = sb("sqscr", [128, 2048], BF16)
    rn = sb("rn", [128, 64], F32)

    C_SUM, C_SSQ, C_MEAN, C_VAR, C_R, C_B, C_T0 = 0, 1, 2, 3, 4, 5, 6

    R = lambda name: Res(name)
    r_s = R("s")
    r_sc = R("sc")
    r_cst = R("cst")
    r_ident = R("ident")
    r_woutT = R("woutT")
    r_kvblk = R("kvblk")
    r_ksumblk = R("ksumblk")
    r_stats = [R(f"stats{t}") for t in range(4)]
    r_sq = R("sq")
    r_rn = [R(f"rn{t}") for t in range(4)]
    r_qscr = [[R(f"qscr{st}_{oc}") for oc in range(8)] for st in range(NST2)]
    r_w0scr = [R(f"w0scr{k}") for k in range(8)]
    r_w1scr = [R(f"w1scr{k}") for k in range(32)]
    r_w2scr = [R(f"w2scr{k}") for k in range(32)]
    r_out = R("out")

    def ln_core(tc, src_ap, r_src, src_banks=(), pre_fn=None, n_feat=1024,
                extra_writes=()):
        st_ap = stats[:, tc * 16:(tc + 1) * 16]
        rs = r_stats[tc]
        er = list(src_banks)
        inv = 1.0 / n_feat

        def vfn(e):
            if pre_fn is not None:
                pre_fn(e)
            e.scalar_tensor_tensor(sqscr[:, 0:n_feat], src_ap, 1.0, src_ap,
                                   ALU.bypass, ALU.mult,
                                   accum_out=st_ap[:, C_SSQ:C_SSQ + 1])
            e.reduce_sum(st_ap[:, C_SUM:C_SUM + 1], src_ap,
                         axis=mybir.AxisListType.X, negate=True)
            e.drain()
            e.tensor_scalar_mul(st_ap[:, C_MEAN:C_MEAN + 1],
                                st_ap[:, C_SUM:C_SUM + 1], inv)
            e.drain()
            e.tensor_tensor(st_ap[:, C_T0:C_T0 + 1], st_ap[:, C_MEAN:C_MEAN + 1],
                            st_ap[:, C_MEAN:C_MEAN + 1], ALU.mult)
            e.drain()
            return e.scalar_tensor_tensor(
                st_ap[:, C_VAR:C_VAR + 1], st_ap[:, C_SSQ:C_SSQ + 1], inv,
                st_ap[:, C_T0:C_T0 + 1], ALU.mult, ALU.subtract)
        mk.group("V", vfn, reads=[r_src] + er,
                 writes=[rs] + list(extra_writes))

        def sfn(e):
            e.activation(st_ap[:, C_T0:C_T0 + 1], st_ap[:, C_VAR:C_VAR + 1],
                         AF.Ln, bias=cst[:, 0:1])
            e.activation(st_ap[:, C_R:C_R + 1], st_ap[:, C_T0:C_T0 + 1],
                         AF.Exp, scale=cst[:, 1:2])
            e.drain()
            e.activation(st_ap[:, C_B:C_B + 1], st_ap[:, C_MEAN:C_MEAN + 1],
                         AF.Identity, scale=st_ap[:, C_R:C_R + 1])
            return e.drain()
        mk.group("S", sfn, reads=[rs, r_cst], writes=[rs])
        return st_ap

    def ln_core_s(tc, src_ap, r_src, src_banks=(), pre_fn=None,
                  extra_writes=(), n_feat=1024, sum_on_v=False):
        st_ap = stats[:, tc * 16:(tc + 1) * 16]
        rs = r_stats[tc]
        er = list(src_banks)

        if sum_on_v:
            mk.group("V", lambda e: e.reduce_sum(
                st_ap[:, C_SUM:C_SUM + 1], src_ap,
                axis=mybir.AxisListType.X, negate=True),
                reads=[r_src] + er, writes=[rs])

        def sfn(e):
            if pre_fn is not None:
                pre_fn(e)
            e.activation(sqscr[:, 1024:1024 + n_feat], src_ap, AF.Square,
                         accum_out=st_ap[:, C_SSQ:C_SSQ + 1])
            if not sum_on_v:
                e.activation(sqscr[:, 1024:1024 + n_feat], src_ap, AF.Identity,
                             scale=cst[:, 2:3],
                             accum_out=st_ap[:, C_SUM:C_SUM + 1])
            e.drain()
            e.activation(st_ap[:, C_MEAN:C_MEAN + 1], st_ap[:, C_SUM:C_SUM + 1],
                         AF.Identity, scale=cst[:, 4:5])
            e.activation(st_ap[:, C_T0:C_T0 + 1], st_ap[:, C_MEAN:C_MEAN + 1],
                         AF.Square)
            e.activation(st_ap[:, C_VAR:C_VAR + 1], st_ap[:, C_T0:C_T0 + 1],
                         AF.Identity, scale=cst[:, 2:3])
            e.drain()
            e.activation(st_ap[:, C_T0:C_T0 + 1], st_ap[:, C_SSQ:C_SSQ + 1],
                         AF.Identity, scale=cst[:, 4:5],
                         bias=st_ap[:, C_VAR:C_VAR + 1])
            e.activation(st_ap[:, C_VAR:C_VAR + 1], st_ap[:, C_T0:C_T0 + 1],
                         AF.Ln, bias=cst[:, 0:1])
            e.activation(st_ap[:, C_R:C_R + 1], st_ap[:, C_VAR:C_VAR + 1],
                         AF.Exp, scale=cst[:, 1:2])
            e.drain()
            e.activation(st_ap[:, C_B:C_B + 1], st_ap[:, C_MEAN:C_MEAN + 1],
                         AF.Identity, scale=st_ap[:, C_R:C_R + 1])
            return e.drain()
        mk.group("S", sfn, reads=[r_src, r_cst] + er,
                 writes=[rs] + list(extra_writes))

    def ln_core_presum(tc, src_ap, r_src, n_feat=1024):
        st_ap = stats[:, tc * 16:(tc + 1) * 16]
        rs = r_stats[tc]
        inv = 1.0 / n_feat

        def vfn(e):
            e.scalar_tensor_tensor(sqscr[:, 0:n_feat], src_ap, 1.0, src_ap,
                                   ALU.bypass, ALU.mult,
                                   accum_out=st_ap[:, C_SSQ:C_SSQ + 1])
            e.drain()
            e.tensor_tensor(st_ap[:, C_MEAN:C_MEAN + 1],
                            st_ap[:, C_SUM:C_SUM + 1], cst[:, 5:6], ALU.mult)
            e.drain()
            e.tensor_tensor(st_ap[:, C_T0:C_T0 + 1], st_ap[:, C_MEAN:C_MEAN + 1],
                            st_ap[:, C_MEAN:C_MEAN + 1], ALU.mult)
            e.drain()
            return e.scalar_tensor_tensor(
                st_ap[:, C_VAR:C_VAR + 1], st_ap[:, C_SSQ:C_SSQ + 1], inv,
                st_ap[:, C_T0:C_T0 + 1], ALU.mult, ALU.subtract)
        mk.group("V", vfn, reads=[r_src, rs, r_cst], writes=[rs])

        def sfn(e):
            e.activation(st_ap[:, C_T0:C_T0 + 1], st_ap[:, C_VAR:C_VAR + 1],
                         AF.Ln, bias=cst[:, 0:1])
            e.activation(st_ap[:, C_R:C_R + 1], st_ap[:, C_T0:C_T0 + 1],
                         AF.Exp, scale=cst[:, 1:2])
            e.drain()
            e.activation(st_ap[:, C_B:C_B + 1], st_ap[:, C_MEAN:C_MEAN + 1],
                         AF.Identity, scale=st_ap[:, C_R:C_R + 1])
            return e.drain()
        mk.group("S", sfn, reads=[rs, r_cst], writes=[rs])

    def ln_any(tc, src_ap, r_src, on_s, src_banks=(), pre_fn=None,
               extra_writes=(), n_feat=1024):
        if on_s:
            ln_core_s(tc, src_ap, r_src, src_banks=src_banks, pre_fn=pre_fn,
                      extra_writes=extra_writes, n_feat=n_feat)
        else:
            ln_core(tc, src_ap, r_src, src_banks=src_banks, pre_fn=pre_fn,
                    extra_writes=extra_writes, n_feat=n_feat)

    def ln_apply_s(tc, dst_ap, src_ap, r_src, r_dst, src_banks=()):
        st_ap = stats[:, tc * 16:(tc + 1) * 16]
        mk.group("S", lambda e: e.activation(
            dst_ap, src_ap, AF.Identity,
            bias=st_ap[:, C_B:C_B + 1], scale=st_ap[:, C_R:C_R + 1]),
            reads=[r_src, r_stats[tc]] + list(src_banks), writes=[r_dst])

    def ln_apply_v(tc, dst_ap, src_ap, r_src, r_dst, src_banks=()):
        st_ap = stats[:, tc * 16:(tc + 1) * 16]
        mk.group("V", lambda e: e.tensor_scalar(
            dst_ap, src_ap, st_ap[:, C_MEAN:C_MEAN + 1],
            st_ap[:, C_R:C_R + 1], ALU.add, ALU.mult),
            reads=[r_src, r_stats[tc]] + list(src_banks), writes=[r_dst])

    def ln_apply_any(tc, dst_ap, src_ap, r_src, r_dst, on_v, src_banks=()):
        if on_v:
            ln_apply_v(tc, dst_ap, src_ap, r_src, r_dst, src_banks)
        else:
            ln_apply_s(tc, dst_ap, src_ap, r_src, r_dst, src_banks)

    tr_rr = [0]

    def transpose8(src_tile_ap, r_src, tr_ps, tr_bank_pair, dst_ap_fn, r_dst):
        g = tr_rr[0] % 2
        tr_rr[0] += 1
        bank = tr_bank_pair[g]
        def tfn(e, g=g):
            last = None
            for j in range(8):
                last = e.transpose(
                    tr_ps[:, g * 1024 + j * 128: g * 1024 + (j + 1) * 128],
                    src_tile_ap[:, j * 128:(j + 1) * 128], identb[:])
            return last
        mk.group("T", tfn, reads=[r_src, r_ident], writes=[bank])
        mk.group("V", lambda e, g=g: e.tensor_copy(
            dst_ap_fn(None),
            tr_ps[:, g * 1024:(g + 1) * 1024]
            .rearrange("p (j c) -> p j c", c=128)),
            reads=[bank], writes=[r_dst])

    p1 = ExitStack()
    p1sb = lambda name, shape, dt: p1.enter_context(nc.sbuf_tensor(name, shape, dt))
    wqkvT = p1sb("wqkvT", [128, 8 * 3072], BF16)
    wtmp = p1sb("wtmp", [128, 4096], F32)
    ku8 = p1sb("ku8", [128, 2 * 4096], U8)
    kb8 = p1sb("kb8", [128, 4096], U8)
    wcvt = p1sb("wcvt", [128, 2 * 4096], BF16)
    x1 = p1sb("x1", [128, 3 * 4096], F32)
    ln0f = p1sb("ln0f", [128, 2 * 4096], BF16)
    ln0T = p1sb("ln0T", [128, 4096], BF16)
    kvt = p1sb("kvt", [128, 1024], BF16)
    vext = p1sb("vext", [128, 1040], BF16)
    elu1 = p1sb("elu1", [128, 2 * 512], BF16)
    elu2 = p1sb("elu2", [128, 2 * 512], BF16)
    qtile = p1sb("qtile", [128, 4 * 512], BF16)

    ps1 = p1.enter_context(nc.psum_tensor("ps1", [128, 4096], F32))
    kv_ps = ps1[:, 0:1536]
    kv_banks = mk.banks(0, 3)
    proj_ps = ps1[:, 1536:3072]
    proj_banks = mk.banks(3, 6)
    tr_ps = ps1[:].bitcast(BF16)[:, 2 * 3072: 2 * 3072 + 2048]
    tr_banks = mk.banks(6, 8)

    r_wqkvT = R("wqkvT")
    r_wtmp = [R(f"wtmp{i}") for i in range(4)]
    r_ku8 = [R("ku80"), R("ku81")]
    r_kb8 = R("kb8")
    r_wcvt = [R("wcvt0"), R("wcvt1")]
    r_x1 = [R("x10"), R("x11"), R("x12")]
    r_ln0f = [[R(f"ln0f{p}_{t}") for t in range(4)] for p in range(2)]
    r_ln0T = R("ln0T")
    r_kvt = R("kvt")
    r_vext = R("vext")
    r_elu1 = [R("elu1a"), R("elu1b")]
    r_elu2 = [R("elu2a"), R("elu2b")]
    r_qtile = [R(f"qtile{i}") for i in range(4)]
    proj_rr = [0]
    elu_rr = [0]

    def kv_pair_col(j):
        return (j // 3) * 512 + (j % 3) * 130

    mk.new_block()
    mk.group("Q", lambda e, dma: (
        dma(identf[:], identf_dr.ap()[:, :]),
        dma(identb[:], identb_dr.ap()[:, :]),
        dma(s_sb[:], s_dr[:, :]),
    ), writes=[r_ident, r_s], dma_n=3)
    mk.group("V", lambda e: e.memset(ones1[:], 1.0), writes=[r_ident])
    early_x0 = [False]
    mk.group("T", lambda e: e.matmul(proj_ps[:, 0:4], ones1[:], s_sb[:],
                                     start=True, stop=True),
             reads=[r_s, r_ident], writes=[proj_banks[0]])
    mk.group("V", lambda e: e.tensor_scalar_mul(sc[:, 0:4], proj_ps[:, 0:4], 2.0),
             reads=[proj_banks[0]], writes=[r_sc])

    def init_fn(e):
        e.memset(kvblk[:], 0.0)
        e.memset(ksumblk[:], 0.0)
        e.memset(cst[:, 0:1], EPS)
        e.memset(cst[:, 1:2], -0.5)
        e.memset(cst[:, 2:3], -1.0)
        e.memset(cst[:, 3:4], 2.0)
        e.memset(cst[:, 4:5], 1.0 / 1024)
        e.memset(cst[:, 5:6], -1.0 / 1024)
        return e.memset(
            vext.ap().rearrange("p (h c) -> p h c", c=65)[:, :, 0:1], 1.0)
    mk.group("V", init_fn, writes=[r_kvblk, r_ksumblk, r_cst, r_vext])

    def prep_T(dr, n_oc, dstT, r_dst, dst_stride):
        for oc in range(n_oc):
            par = oc % 4
            mk.group("Q", lambda e, dma, oc=oc, par=par: dma(
                wtmp[:, par * 1024:(par + 1) * 1024],
                dr[oc * 128:(oc + 1) * 128, :]),
                reads=[r_wtmp[par]], writes=[r_wtmp[par]], dma_n=1)
            for grp in range(2):
                bank = proj_banks[grp]
                def tfn(e, oc=oc, par=par, grp=grp):
                    last = None
                    for i in range(4):
                        ii = grp * 4 + i
                        last = e.transpose(
                            proj_ps[:, grp * 512 + i * 128: grp * 512 + (i + 1) * 128],
                            wtmp[:, par * 1024 + ii * 128: par * 1024 + (ii + 1) * 128],
                            identf[:])
                    return last
                mk.group("T", tfn, reads=[r_wtmp[par], r_ident], writes=[bank])
                def efn(e, oc=oc, grp=grp, dstT=dstT, dst_stride=dst_stride):
                    return e.tensor_copy(
                        dstT.ap().rearrange("p (i c) -> p i c", c=dst_stride)
                        [:, grp * 4:(grp + 1) * 4, oc * 128:(oc + 1) * 128],
                        proj_ps[:, grp * 512:(grp + 1) * 512]
                        .rearrange("p (i c) -> p i c", c=128))
                mk.group("V", efn, reads=[bank], writes=[r_dst])

    def load_x1(st):
        par = st % 3
        mk.group("Q", lambda e, dma, st=st, par=par: [
            dma(x1[:, par * 4096 + tc * 1024: par * 4096 + (tc + 1) * 1024],
                x_dr[st * 512 + tc * 128: st * 512 + (tc + 1) * 128, :])
            for tc in range(4)][-1],
            reads=[r_x1[par]], writes=[r_x1[par]], dma_n=4)

    load_x1(0)
    prep_T(wqkv_dr, 24, wqkvT, r_wqkvT, 3072)
    prep_T(wout_dr, 8, woutT, r_woutT, 1024)

    prep_units = ([("w0", k) for k in range(8)]
                  + [("w1", k) for k in range(32)]
                  + [("w2", k) for k in range(32)])
    prep_pos = [0]

    def emit_prep_units(n):
        for _ in range(n):
            if prep_pos[0] >= len(prep_units):
                return
            which, k = prep_units[prep_pos[0]]
            prep_pos[0] += 1
            par = prep_pos[0] % 2
            if which == "w0":
                mk.group("Q", lambda e, dma, k=k, par=par: (
                    dma(ku8[:, par * 4096:(par + 1) * 4096],
                        k0_dr[k * 128:(k + 1) * 128, :]),
                    dma(kb8[:],
                        k0_dr[1024 + k * 128: 1024 + (k + 1) * 128, :]),
                ), reads=[r_ku8[par], r_kb8],
                    writes=[r_ku8[par], r_kb8], dma_n=2)
                mk.group("G", lambda e, par=par: e.tensor_tensor(
                    wcvt[:, par * 4096:(par + 1) * 4096],
                    ku8[:, par * 4096:(par + 1) * 4096], kb8[:], ALU.subtract),
                    reads=[r_ku8[par], r_kb8], writes=[r_wcvt[par]])
                mk.group("G", lambda e, dma, k=k, par=par: dma(
                    w0scr[k * 128:(k + 1) * 128, :],
                    wcvt[:, par * 4096:(par + 1) * 4096]),
                    reads=[r_wcvt[par]], writes=[r_w0scr[k]], dma_n=1)
                continue
            src, dst, rdst, w = (
                (k1_dr, w1scr, r_w1scr[k], 4096) if which == "w1"
                else (k2_dr, w2scr, r_w2scr[k], 1024))
            mk.group("Q", lambda e, dma, src=src, k=k, par=par, w=w: dma(
                ku8[:, par * 4096: par * 4096 + w],
                src[k * 128:(k + 1) * 128, :]),
                reads=[r_ku8[par]], writes=[r_ku8[par]], dma_n=1)
            mk.group("G", lambda e, par=par, w=w: e.tensor_scalar(
                wcvt[:, par * 4096: par * 4096 + w],
                ku8[:, par * 4096: par * 4096 + w], 2.0, -1.0,
                ALU.mult, ALU.add),
                reads=[r_ku8[par]], writes=[r_wcvt[par]])
            mk.group("G", lambda e, dma, dst=dst, k=k, par=par, w=w: dma(
                dst[k * 128:(k + 1) * 128, :],
                wcvt[:, par * 4096: par * 4096 + w]),
                reads=[r_wcvt[par]], writes=[rdst], dma_n=1)

    def ln_chain1(st, tc):
        par = st % 2
        xpar = st % 3
        xs = x1[:, xpar * 4096 + tc * 1024: xpar * 4096 + (tc + 1) * 1024]
        ln_any(tc, xs, r_x1[xpar], on_s=(tc % 2 == 0))
        ln_apply_s(tc, ln0f[:, par * 4096 + tc * 1024:
                           par * 4096 + (tc + 1) * 1024], xs,
                   r_x1[xpar], r_ln0f[par][tc])

    def ln_chains1(st):
        for tc in range(4):
            ln_chain1(st, tc)

    first_kv = [True]
    ln_chains1(0)
    for st in range(NST1):
        par = st % 2
        own = st < NST2

        if st + 1 < NST1:
            load_x1(st + 1)
        emit_prep_units(5 if st > 0 else 0)

        for tc in range(4):
            transpose8(
                ln0f[:, par * 4096 + tc * 1024: par * 4096 + (tc + 1) * 1024],
                r_ln0f[par][tc], tr_ps, tr_banks,
                lambda g, tc=tc: ln0T.ap()
                .rearrange("p (j c) -> p j c", c=512)
                [:, :, tc * 128:(tc + 1) * 128],
                r_ln0T)

            for ot in range(4):
                pj = proj_rr[0] % 3
                proj_rr[0] += 1
                pb = proj_banks[pj]
                pslice = proj_ps[:, pj * 512:(pj + 1) * 512]
                def mmfn(e, tc=tc, ot=ot, pslice=pslice):
                    last = None
                    for j in range(8):
                        last = e.matmul(
                            pslice,
                            ln0T[:, j * 512 + tc * 128: j * 512 + (tc + 1) * 128],
                            wqkvT[:, j * 3072 + 1024 + ot * 512:
                                  j * 3072 + 1024 + (ot + 1) * 512],
                            start=(j == 0), stop=(j == 7))
                    return last
                mk.group("T", mmfn, reads=[r_ln0T, r_wqkvT], writes=[pb])
                if ot < 2:
                    ep = elu_rr[0] % 2
                    elu_rr[0] += 1
                    e1 = elu1[:, ep * 512:(ep + 1) * 512]
                    e2 = elu2[:, ep * 512:(ep + 1) * 512]
                    mk.group("V", lambda e, pslice=pslice, e1=e1: e.tensor_scalar(
                        e1, pslice, 0.0, None, ALU.min),
                        reads=[pb], writes=[r_elu1[ep]])
                    mk.group("S", lambda e, e1=e1, e2=e2: e.activation(
                        e2, e1, AF.Exp),
                        reads=[r_elu1[ep]], writes=[r_elu2[ep]])
                    mk.group("V", lambda e, ot=ot, pslice=pslice, e2=e2:
                             e.scalar_tensor_tensor(
                                 kvt[:, ot * 512:(ot + 1) * 512], pslice, 0.0,
                                 e2, ALU.max, ALU.add),
                             reads=[pb, r_elu2[ep]], writes=[r_kvt])
                else:
                    hbase = (ot - 2) * 8
                    mk.group("V", lambda e, hbase=hbase, pslice=pslice:
                             e.tensor_copy(
                                 vext.ap().rearrange("p (h c) -> p h c", c=65)
                                 [:, hbase:hbase + 8, 1:65],
                                 pslice.rearrange("p (h c) -> p h c", c=64)),
                             reads=[pb], writes=[r_vext])
            is_last_kv = (st == NST1 - 1 and tc == 3)
            def kvfn(e, first=first_kv[0], last_kv=is_last_kv):
                last = None
                for j in range(8):
                    last = e.matmul(
                        kv_ps[:, kv_pair_col(j): kv_pair_col(j) + 130],
                        kvt[:, j * 128:(j + 1) * 128],
                        vext[:, j * 130:(j + 1) * 130],
                        start=(first and j % 3 == 0), stop=last_kv)
                return last
            mk.group("T", kvfn, reads=[r_kvt, r_vext], writes=list(kv_banks))
            first_kv[0] = False
            if st + 1 < NST1:
                ln_chain1(st + 1, tc)

        if own:
            stq = st
            for oc in range(8):
                pj = proj_rr[0] % 3
                proj_rr[0] += 1
                pb = proj_banks[pj]
                pslice = proj_ps[:, pj * 512:(pj + 1) * 512]
                qpar = oc % 4
                def qmm(e, oc=oc, pslice=pslice):
                    last = None
                    for j in range(8):
                        last = e.matmul(
                            pslice,
                            wqkvT[:, j * 3072 + oc * 128: j * 3072 + (oc + 1) * 128],
                            ln0T[:, j * 512:(j + 1) * 512],
                            start=(j == 0), stop=(j == 7))
                    return last
                mk.group("T", qmm, reads=[r_ln0T, r_wqkvT], writes=[pb])
                ep = elu_rr[0] % 2
                elu_rr[0] += 1
                e1 = elu1[:, ep * 512:(ep + 1) * 512]
                e2 = elu2[:, ep * 512:(ep + 1) * 512]
                mk.group("V", lambda e, pslice=pslice, e1=e1: e.tensor_scalar(
                    e1, pslice, 0.0, None, ALU.min),
                    reads=[pb], writes=[r_elu1[ep]])
                mk.group("S", lambda e, e1=e1, e2=e2: e.activation(e2, e1, AF.Exp),
                         reads=[r_elu1[ep]], writes=[r_elu2[ep]])
                mk.group("V", lambda e, qpar=qpar, pslice=pslice, e2=e2:
                         e.scalar_tensor_tensor(
                             qtile[:, qpar * 512:(qpar + 1) * 512], pslice, 0.0,
                             e2, ALU.max, ALU.add),
                         reads=[pb, r_elu2[ep]], writes=[r_qtile[qpar]])
                mk.group("Q", lambda e, dma, stq=stq, oc=oc, qpar=qpar: dma(
                    qscr[(stq * 8 + oc) * 128:(stq * 8 + oc + 1) * 128, :],
                    qtile[:, qpar * 512:(qpar + 1) * 512]),
                    reads=[r_qtile[qpar]], writes=[r_qscr[stq][oc]], dma_n=1)

    emit_prep_units(len(prep_units))

    def kvevict(e):
        last = None
        for j in range(8):
            c = kv_pair_col(j)
            last = e.tensor_copy(kvblk[0:64, j * 128: j * 128 + 64],
                                 kv_ps[0:64, c + 1: c + 65])
            last = e.tensor_copy(kvblk[64:128, j * 128 + 64: j * 128 + 128],
                                 kv_ps[64:128, c + 66: c + 130])
            last = e.tensor_copy(ksumblk[0:64, 2 * j: 2 * j + 1],
                                 kv_ps[0:64, c: c + 1])
            last = e.tensor_copy(ksumblk[64:128, 2 * j + 1: 2 * j + 2],
                                 kv_ps[64:128, c + 65: c + 66])
        return last
    mk.group("V", kvevict, reads=list(kv_banks), writes=[r_kvblk, r_ksumblk])

    if dbg is not None and "phase1_end" in dbg:
        dbg["phase1_end"](nc, mk, locals())

    mk.wait_all("Q", "T")
    mk.wait_all("Q", "V")
    mk.wait_all("Q", "S")
    mk.wait_all("Q", "Qd")
    mk.wait_all("Q", "Sd")

    mk.emit_pending()
    p1.close()
    mk.bank_res = [Res(f"bank{i}") for i in range(8)]

    p2 = ExitStack()
    p2sb = lambda name, shape, dt: p2.enter_context(nc.sbuf_tensor(name, shape, dt))
    x2 = p2sb("x2", [128, 2 * 4096], F32)
    q2 = p2sb("q2", [128, 2 * 4096], BF16)
    atm = p2sb("atm", [128, 2 * 1024], BF16)
    afm = p2sb("afm", [128, 4096], BF16)
    tln = p2sb("tln", [128, 3 * 1024], BF16)
    ln2T = p2sb("ln2T", [128, 4096], BF16)
    h0 = p2sb("h0", [128, 16384], BF16)
    h1 = p2sb("h1", [128, 16384], BF16)
    h2b = p2sb("h2b", [128, 4 * 1024], F32)
    outb = p2sb("outb", [128, 2 * 1024], F32)
    ybuf = p2sb("ybuf", [128, 2 * 1024], F32)
    wring = p2sb("wring", [128, 3 * 2048], BF16)

    ps2 = p2.enter_context(nc.psum_tensor("ps2", [128, 4096], F32))
    att_psp = [ps2[:, 0:1024], ps2[:, 1024:2048]]
    att_banksp = [mk.banks(0, 2), mk.banks(2, 4)]
    y_psp = att_psp
    y_banksp = att_banksp
    norm_psp = [ps2[:, 2048:2560], ps2[:, 3584:4096]]
    norm_banksp = [mk.banks(4, 5), mk.banks(7, 8)]
    trp_ps = ps2[:].bitcast(BF16)[:, 2 * 2560: 2 * 2560 + 2048]
    trp_banks = mk.banks(5, 7)
    ffn_ps = ps2
    ffn_banks = mk.banks(0, 8)

    r_x2 = [R("x20"), R("x21")]
    r_q2 = [R("q20"), R("q21")]
    r_atm = [R("atm0"), R("atm1")]
    r_afm = [R(f"afm{t}") for t in range(4)]
    r_tln = [R(f"tln{i}") for i in range(3)]
    r_ln2T = R("ln2T")
    r_h0 = [R("h0a"), R("h0b")]
    r_h1 = [R("h1a"), R("h1b")]
    r_h2b = [R(f"h2b{i}") for i in range(4)]
    r_outb = [R("outb0"), R("outb1")]
    r_ybuf = [R("ybuf0"), R("ybuf1")]
    r_wring = [R(f"wring{i}") for i in range(6)]

    mk.new_block()

    def load_xq(st):
        par = st % 2
        mk.group("Q", lambda e, dma, st=st, par=par: [
            dma(x2[:, par * 4096 + tc * 1024: par * 4096 + (tc + 1) * 1024],
                x_dr[st * 512 + tc * 128: st * 512 + (tc + 1) * 128, :])
            for tc in range(4)][-1],
            reads=[r_x2[par]], writes=[r_x2[par]], dma_n=4)
        mk.group("Q", lambda e, dma, st=st, par=par: [
            dma(q2[:, par * 4096 + j * 512: par * 4096 + (j + 1) * 512],
                qscr[(st * 8 + j) * 128:(st * 8 + j + 1) * 128, :])
            for j in range(8)][-1],
            reads=[r_q2[par]] + r_qscr[st], writes=[r_q2[par]], dma_n=8)

    load_xq(0)
    pending_tail = []

    for st in range(NST2):
        par = st % 2
        def attT_emit(tc):
            tpar = tc % 2
            transpose8(
                atm[:, tpar * 1024:(tpar + 1) * 1024], r_atm[tpar],
                trp_ps, trp_banks,
                lambda g, tc=tc: afm.ap()
                .rearrange("p (j c) -> p j c", c=512)
                [:, :, tc * 128:(tc + 1) * 128],
                r_afm[tc])

        def ln2T_emit(tc):
            tsl = tc % 3
            transpose8(
                tln[:, tsl * 1024:(tsl + 1) * 1024], r_tln[tsl],
                trp_ps, trp_banks,
                lambda g, tc=tc: ln2T.ap()
                .rearrange("p (j c) -> p j c", c=512)
                [:, :, tc * 128:(tc + 1) * 128],
                r_ln2T)

        for tc in range(4):
            tpar = tc % 2
            norm_ps = norm_psp[tpar]
            norm_banks = norm_banksp[tpar]
            def nfn(e, tc=tc, par=par, norm_ps=norm_ps):
                last = None
                for j in range(8):
                    last = e.matmul(
                        norm_ps[:, tc * 16 + 2 * j: tc * 16 + 2 * j + 2],
                        q2[:, par * 4096 + j * 512 + tc * 128:
                           par * 4096 + j * 512 + (tc + 1) * 128],
                        ksumblk[:, 2 * j: 2 * j + 2],
                        start=True, stop=True)
                return last
            mk.group("T", nfn, reads=[r_q2[par], r_ksumblk], writes=list(norm_banks))
            att_ps = att_psp[tpar]
            def afn(e, tc=tc, par=par, att_ps=att_ps):
                last = None
                for j in range(8):
                    last = e.matmul(
                        att_ps[:, j * 128:(j + 1) * 128],
                        q2[:, par * 4096 + j * 512 + tc * 128:
                           par * 4096 + j * 512 + (tc + 1) * 128],
                        kvblk[:, j * 128:(j + 1) * 128],
                        start=True, stop=True)
                return last
            mk.group("T", afn, reads=[r_q2[par], r_kvblk],
                     writes=list(att_banksp[tpar]))
            def rnfn(e, tc=tc, norm_ps=norm_ps):
                e.tensor_scalar(rn[:, tc * 16:(tc + 1) * 16],
                                norm_ps[:, tc * 16:(tc + 1) * 16],
                                1e-6, None, ALU.max)
                e.drain()
                last = e.reciprocal(rn[:, tc * 16:(tc + 1) * 16],
                                    rn[:, tc * 16:(tc + 1) * 16])
                e.drain()
                return last
            mk.group("V", rnfn, reads=list(norm_banks), writes=[r_rn[tc]])
            mk.group("V", lambda e, tc=tc, tpar=tpar, att_ps=att_ps:
                e.tensor_tensor(
                    atm[:, tpar * 1024:(tpar + 1) * 1024]
                    .rearrange("p (h c) -> p h c", c=64),
                    att_ps.rearrange("p (h c) -> p h c", c=64),
                    rn[:, tc * 16:(tc + 1) * 16].unsqueeze(2)
                    .broadcast_to([128, 16, 64]),
                    ALU.mult),
                reads=list(att_banksp[tpar]) + [r_rn[tc]], writes=[r_atm[tpar]])
            if tc >= 1:
                attT_emit(tc - 1)
        attT_emit(2)
        attT_emit(3)

        for fn_ in pending_tail:
            fn_()
        pending_tail.clear()

        if st + 1 < NST2:
            load_xq(st + 1)

        for tc in range(4):
            tpar = tc % 2
            y_ps = y_psp[tpar]
            def yfn(e, tc=tc, y_ps=y_ps):
                last = None
                for oh in range(2):
                    for j in range(8):
                        last = e.matmul(
                            y_ps[:, oh * 512:(oh + 1) * 512],
                            afm[:, j * 512 + tc * 128: j * 512 + (tc + 1) * 128],
                            woutT[:, j * 1024 + oh * 512: j * 1024 + (oh + 1) * 512],
                            start=(j == 0), stop=(j == 7))
                return last
            mk.group("T", yfn, reads=[r_afm[tc], r_woutT],
                     writes=list(y_banksp[tpar]))
            yb = ybuf[:, tpar * 1024:(tpar + 1) * 1024]
            st_ap = stats[:, tc * 16:(tc + 1) * 16]
            xs = x2[:, par * 4096 + tc * 1024: par * 4096 + (tc + 1) * 1024]
            mk.group("S", lambda e, yb=yb, y_ps=y_ps, st_ap=st_ap:
                     e.activation(yb, y_ps, AF.Identity,
                                  accum_out=st_ap[:, C_SUM:C_SUM + 1]),
                     reads=list(y_banksp[tpar]),
                     writes=[r_ybuf[tpar], r_stats[tc]])
            ln_core_presum(tc, yb, r_ybuf[tpar])
            def add1(e, xs=xs, yb=yb, st_ap=st_ap):
                e.tensor_scalar(yb, yb, st_ap[:, C_MEAN:C_MEAN + 1],
                                st_ap[:, C_R:C_R + 1], ALU.add, ALU.mult)
                return e.tensor_add(xs, xs, yb)
            mk.group("V", add1,
                     reads=[r_ybuf[tpar], r_stats[tc], r_x2[par]],
                     writes=[r_x2[par], r_ybuf[tpar]])
            ln_core(tc, xs, r_x2[par])
            tsl = tc % 3
            ln_apply_s(tc, tln[:, tsl * 1024:(tsl + 1) * 1024], xs,
                       r_x2[par], r_tln[tsl])
            if tc >= 2:
                ln2T_emit(tc - 2)
        ln2T_emit(2)
        ln2T_emit(3)

        def ffn_layer(nk, wsrc_scr, r_wscr, rhs_sb, r_rhs, dst_sb, r_dst,
                      scale_col, relu, wq):
            for ocb in range(8):
                pb = (ocb % 2) * 4
                for kp in range(nk // 2):
                    ring = (ocb * (nk // 2) + kp) % 6
                    k0_ = 2 * kp
                    mk.group(wq, lambda e, dma, k0_=k0_, ocb=ocb, ring=ring,
                             wsrc_scr=wsrc_scr: dma(
                                 wring[:, ring * 1024:(ring + 1) * 1024]
                                 .rearrange("p (a c) -> p a c", c=512),
                                 wsrc_scr[k0_ * 128:(k0_ + 2) * 128,
                                          ocb * 512:(ocb + 1) * 512]
                                 .rearrange("(a r) c -> r a c", r=128)),
                             reads=[r_wring[ring], r_wscr[k0_], r_wscr[k0_ + 1]],
                             writes=[r_wring[ring]], dma_n=1)
                    def mmfn(e, k0_=k0_, ring=ring, nk=nk, rhs_sb=rhs_sb,
                             pb=pb):
                        last = None
                        for half in range(2):
                            k = k0_ + half
                            for oc in range(4):
                                last = e.matmul(
                                    ffn_ps[:, (pb + oc) * 512:
                                           (pb + oc + 1) * 512],
                                    wring[:, ring * 1024 + half * 512 + oc * 128:
                                          ring * 1024 + half * 512 + (oc + 1) * 128],
                                    rhs_sb[:, k * 512:(k + 1) * 512],
                                    start=(k == 0), stop=(k == nk - 1))
                        return last
                    mk.group("T", mmfn,
                             reads=[r_wring[ring]] + list(r_rhs),
                             writes=list(ffn_banks[pb:pb + 4]))
                if ocb % 2 == 0:
                    def evfnS(e, ocb=ocb, pb=pb, dst_sb=dst_sb,
                              scale_col=scale_col, relu=relu):
                        last = None
                        for oc in range(4):
                            last = e.activation(
                                dst_sb[:, (ocb * 4 + oc) * 512:
                                       (ocb * 4 + oc + 1) * 512],
                                ffn_ps[:, (pb + oc) * 512:(pb + oc + 1) * 512],
                                AF.Relu if relu else AF.Copy,
                                scale=sc[:, scale_col:scale_col + 1])
                        return last
                    mk.group("S", evfnS, reads=list(ffn_banks[pb:pb + 4]) + [r_sc],
                             writes=[r_dst[0]])
                else:
                    def evfnV(e, ocb=ocb, pb=pb, dst_sb=dst_sb,
                              scale_col=scale_col, relu=relu):
                        last = None
                        for oc in range(4):
                            if relu:
                                last = e.tensor_scalar(
                                    dst_sb[:, (ocb * 4 + oc) * 512:
                                           (ocb * 4 + oc + 1) * 512],
                                    ffn_ps[:, (pb + oc) * 512:(pb + oc + 1) * 512],
                                    sc[:, scale_col:scale_col + 1], 0.0,
                                    ALU.mult, ALU.max)
                            else:
                                last = e.tensor_scalar_mul(
                                    dst_sb[:, (ocb * 4 + oc) * 512:
                                           (ocb * 4 + oc + 1) * 512],
                                    ffn_ps[:, (pb + oc) * 512:(pb + oc + 1) * 512],
                                    sc[:, scale_col:scale_col + 1])
                        return last
                    mk.group("V", evfnV, reads=list(ffn_banks[pb:pb + 4]) + [r_sc],
                             writes=[r_dst[1]])

        ffn_layer(8, w0scr, r_w0scr, ln2T, [r_ln2T], h0, r_h0, 0, True, "G")
        ffn_layer(32, w1scr, r_w1scr, h0, r_h0, h1, r_h1, 1, True, "Q")

        for kp in range(16):
            ring = kp % 3
            k0_ = 2 * kp
            mk.group("G", lambda e, dma, k0_=k0_, ring=ring: dma(
                wring[:, ring * 2048:(ring + 1) * 2048]
                .rearrange("p (a c) -> p a c", c=1024),
                w2scr[k0_ * 128:(k0_ + 2) * 128, :]
                .rearrange("(a r) c -> r a c", r=128)),
                reads=[r_wring[2 * ring], r_wring[2 * ring + 1],
                       r_w2scr[k0_], r_w2scr[k0_ + 1]],
                writes=[r_wring[2 * ring], r_wring[2 * ring + 1]], dma_n=1)
            def h2fn(e, k0_=k0_, ring=ring):
                last = None
                for half in range(2):
                    k = k0_ + half
                    for tc in range(4):
                        for oh in range(2):
                            last = e.matmul(
                                ffn_ps[:, (tc * 2 + oh) * 512:(tc * 2 + oh + 1) * 512],
                                h1[:, k * 512 + tc * 128: k * 512 + (tc + 1) * 128],
                                wring[:, ring * 2048 + half * 1024 + oh * 512:
                                      ring * 2048 + half * 1024 + (oh + 1) * 512],
                                start=(k == 0), stop=(k == 31))
                return last
            mk.group("T", h2fn,
                     reads=[r_wring[2 * ring], r_wring[2 * ring + 1]] + r_h1,
                     writes=list(ffn_banks))
        def h2ev_s(e):
            e.activation(h2b[:, 0:1024], ffn_ps[:, 0:1024], AF.Copy,
                         scale=sc[:, 2:3])
            return e.activation(h2b[:, 1024:2048], ffn_ps[:, 1024:2048],
                                AF.Copy, scale=sc[:, 2:3])
        mk.group("S", h2ev_s, reads=list(ffn_banks[0:4]) + [r_sc],
                 writes=[r_h2b[0], r_h2b[1]])
        def h2ev_v(e):
            e.tensor_scalar_mul(h2b[:, 2048:3072], ffn_ps[:, 2048:3072],
                                sc[:, 2:3])
            return e.tensor_scalar_mul(h2b[:, 3072:4096], ffn_ps[:, 3072:4096],
                                       sc[:, 2:3])
        mk.group("V", h2ev_v, reads=list(ffn_banks[4:8]) + [r_sc],
                 writes=[r_h2b[2], r_h2b[3]])
        for tc in range(4):
            h2s = h2b[:, tc * 1024:(tc + 1) * 1024]
            ln_core_s(tc, h2s, r_h2b[tc])
            st_ap = stats[:, tc * 16:(tc + 1) * 16]
            xs = x2[:, par * 4096 + tc * 1024: par * 4096 + (tc + 1) * 1024]
            def emit_add3(tc=tc, st=st, par=par, h2s=h2s, st_ap=st_ap, xs=xs):
                tpar = tc % 2
                def add3(e, xs=xs, st_ap=st_ap, tpar=tpar, h2s=h2s):
                    e.tensor_scalar_add(xs, xs, st_ap[:, C_B:C_B + 1])
                    return e.scalar_tensor_tensor(
                        outb[:, tpar * 1024:(tpar + 1) * 1024], h2s,
                        st_ap[:, C_R:C_R + 1], xs, ALU.mult, ALU.add)
                mk.group("V", add3,
                         reads=[r_h2b[tc], r_stats[tc], r_x2[par]],
                         writes=[r_x2[par], r_outb[tpar]])
                mk.group("Q", lambda e, dma, st=st, tc=tc, tpar=tpar: dma(
                    out_dr[st * 512 + tc * 128: st * 512 + (tc + 1) * 128, :],
                    outb[:, tpar * 1024:(tpar + 1) * 1024]),
                    reads=[r_outb[tpar]], dma_n=1)
            pending_tail.append(emit_add3)

    for fn_ in pending_tail:
        fn_()
    pending_tail.clear()

    if dbg is not None and "phase2_end" in dbg:
        dbg["phase2_end"](nc, mk, locals())

    mk.wait_all("Q", "Qd")
    mk.wait_all("Q", "Sd")

    mk.emit_pending()
    p2.close()
    ctx.close()
    return nc


_cache = {}


def kernel(L0_attn_x, L0_attn_w_qkv, L0_attn_w_out, L0_ffn_k0, L0_ffn_k1,
           L0_ffn_k2, L0_ffn_s0, L0_ffn_s1, L0_ffn_s2):
    x = np.ascontiguousarray(np.asarray(L0_attn_x, dtype=np.float32))
    wqkv = np.ascontiguousarray(np.asarray(L0_attn_w_qkv, dtype=np.float32))
    wout = np.ascontiguousarray(np.asarray(L0_attn_w_out, dtype=np.float32))
    k0 = np.ascontiguousarray(np.asarray(L0_ffn_k0)).view(np.uint8)
    k1 = np.ascontiguousarray(np.asarray(L0_ffn_k1)).view(np.uint8)
    k2 = np.ascontiguousarray(np.asarray(L0_ffn_k2)).view(np.uint8)
    s = np.zeros((1, 4), np.float32)
    s[0, 0] = np.asarray(L0_ffn_s0)
    s[0, 1] = np.asarray(L0_ffn_s1)
    s[0, 2] = np.asarray(L0_ffn_s2)

    if "nc" not in _cache:
        _cache["nc"] = build_program()
    nc = _cache["nc"]

    in_maps = []
    for c in range(8):
        b, half = c // 2, c % 2
        own = x[b, half * TOK:(half + 1) * TOK]
        other = x[b, (1 - half) * TOK:(2 - half) * TOK]
        xc = np.concatenate([own, other], axis=0)
        in_maps.append({"x": xc, "wqkv": wqkv, "wout": wout,
                        "k0": k0, "k1": k1, "k2": k2, "s": s})
    res = run_bass_kernel_spmd(nc, in_maps, core_ids=list(range(8)))
    out = np.empty((B, S, D), np.float32)
    for c in range(8):
        b, half = c // 2, c % 2
        out[b, half * TOK:(half + 1) * TOK] = res.results[c]["out"]
    return out
